# version 10
# baseline (speedup 1.0000x reference)
import numpy as np
import concourse.bass as bass
import concourse.mybir as mybir
from concourse.bass_utils import run_bass_kernel_spmd

F32 = mybir.dt.float32
AF = mybir.ActivationFunctionType
ALU = mybir.AluOpType
AX = mybir.AxisListType

D = 1024
DK = 1024
DV = 2048
NH = 4
T_CORE = 2048
CH = 128
NCH = T_CORE // CH
A_TILE = 512
N_AT = T_CORE // A_TILE
CPT = A_TILE // CH
B_TILE = 512
N_BT = T_CORE // B_TILE
BPT = B_TILE // CH
F_TILE = 512
N_FT = T_CORE // F_TILE
I_FFN = 2816
N_IC = I_FFN // 128
EPS = 1e-5


class Eng:
    def __init__(self, name, sem):
        self.name, self.sem, self.cnt = name, sem, 0

    def tok(self):
        return (self, self.cnt)


class Builder:
    def __init__(self, block, sems):
        self.t = Eng("t", sems["t"])
        self.v = Eng("v", sems["v"])
        self.s = Eng("s", sems["s"])
        self.g = Eng("g", sems["g"])
        self.ld = Eng("ld", sems["ld"])
        self.st = Eng("st", sems["st"])
        self.sec = {"t": block.tensor, "v": block.vector, "s": block.scalar,
                    "g": block.gpsimd, "y": block.sync}

    boot = None

    def on(self, eng, deps):
        e = getattr(self, eng)
        if self.boot is not None and not getattr(e, "booted", False):
            deps = list(self.boot) + list(deps)
            e.booted = True
        def deco(fn):
            def _(h):
                for (de, dc) in deps:
                    if dc > 0:
                        h.wait_ge(de.sem, dc)
                fn(h).then_inc(e.sem, 1)
            self.sec[eng](_)
            e.cnt += 1
            return e.tok()
        return deco

    def emit(self, eng, deps, fn):
        return self.on(eng, deps)(fn)

    def _dma(self, se, deps, out, in_):
        def _(h):
            for (de, dc) in deps:
                if dc > 0:
                    h.wait_ge(de.sem, dc)
            h.dma_start(out=out, in_=in_).then_inc(se.sem, 16)
        self.sec["y"](_)
        se.cnt += 16
        return se.tok()

    def load(self, deps, out, in_):
        return self._dma(self.ld, deps, out, in_)

    def store(self, deps, out, in_):
        return self._dma(self.st, deps, out, in_)

    def all_toks(self):
        return [self.t.tok(), self.v.tok(), self.s.tok(), self.g.tok(),
                self.ld.tok(), self.st.tok()]

    def final_wait(self):
        toks = self.all_toks()
        def _(h):
            for (de, dc) in toks:
                if dc > 0:
                    h.wait_ge(de.sem, dc)
        self.sec["y"](_)


class Arena:
    def __init__(self, sb, ncols):
        self.sb, self.ncols, self.off = sb, ncols, 0

    def alloc(self, cols):
        a = self.sb[:, self.off:self.off + cols]
        self.off += cols
        assert self.off <= self.ncols, f"arena overflow {self.off}>{self.ncols}"
        return a

    def reset(self):
        self.off = 0


def build_program(debug=False):
    nc = bass.Bass("TRN2", target_bir_lowering=False)
    dt = nc.dram_tensor

    x_in = dt("x", [T_CORE, D], F32, kind="ExternalInput").ap()
    wqT = dt("wqT", [2, 128, 8 * 512], F32, kind="ExternalInput").ap()
    wkT = dt("wkT", [2, 128, 8 * 512], F32, kind="ExternalInput").ap()
    wvT = dt("wvT", [4, 128, 8 * 512], F32, kind="ExternalInput").ap()
    wgT = dt("wgT", [4, 128, 8 * 512], F32, kind="ExternalInput").ap()
    wgk1T = dt("wgk1T", [D, 32], F32, kind="ExternalInput").ap()
    wgk2Ta = dt("wgk2Ta", [128, DK], F32, kind="ExternalInput").ap()
    woutT = dt("woutT", [2, 128, 8 * 1024], F32, kind="ExternalInput").ap()
    rmsw2 = dt("rmsw2", [128, 16], F32, kind="ExternalInput").ap()
    wg_nat = dt("wg_nat", [I_FFN, D], F32, kind="ExternalInput").ap()
    wu_nat = dt("wu_nat", [I_FFN, D], F32, kind="ExternalInput").ap()
    wd_nat = dt("wd_nat", [D, I_FFN], F32, kind="ExternalInput").ap()
    ctrlneg = dt("ctrlneg", [CH, CH], F32, kind="ExternalInput").ap()
    maskJT = dt("maskJT", [CH, CH], F32, kind="ExternalInput").ap()
    ident = dt("ident", [128, 128], F32, kind="ExternalInput").ap()
    mk_in = dt("mk", [128, 4], F32, kind="ExternalInput").ap()
    mki_in = dt("mki", [128, 4], F32, kind="ExternalInput").ap()
    cv_in = dt("cvals", [128, 8], F32, kind="ExternalInput").ap()

    y_out = dt("y", [T_CORE, D], F32, kind="ExternalOutput").ap()

    wgq_s = dt("wgq_s", [6, 128, 8 * 512], F32).ap()
    wuq_s = dt("wuq_s", [6, 128, 8 * 512], F32).ap()
    wdq_s = dt("wdq_s", [I_FFN, D], F32).ap()
    oloc_s = dt("oloc_s", [T_CORE, DV], F32).ap()
    qeT_s = dt("qeT_s", [NCH * 128, DK], F32).ap()
    gate_s = dt("gate_s", [T_CORE, DV], F32).ap()
    x1_s = dt("x1_s", [T_CORE, D], F32).ap()
    ag_in = dt("ag_in", [128, 4104], F32).ap()
    ag_out = dt("ag_out", [512, 4104], F32).ap()

    dbg = {}
    def mkdbg(name, shape):
        if debug:
            dbg[name] = dt("dbg_" + name, shape, F32, kind="ExternalOutput").ap()
    mkdbg("xch0", [128, D])
    mkdbg("xnch0", [128, D])
    mkdbg("lnst0", [128, 8])
    mkdbg("xnT0", [128, 8 * A_TILE])
    mkdbg("gneg0", [128, CPT * DK])
    mkdbg("q0", [128, 8 * A_TILE])
    mkdbg("v0", [128, CPT * DV])
    mkdbg("ebT0", [128, DK])
    mkdbg("o0", [128, DV])
    mkdbg("S1", [128, 4096])
    mkdbg("U", [128, 4096])
    mkdbg("Sinit", [128, 4096])
    mkdbg("wgq0", [128, I_FFN])
    mkdbg("h2T0", [128, 16 * 512])
    mkdbg("x1c0", [128, 4 * D])
    mkdbg("xq0", [128, D])
    mkdbg("x1f0", [128, D])
    mkdbg("xn2c0", [128, D])
    mkdbg("hq0", [128, 512])
    mkdbg("hb0", [128, 512])
    mkdbg("hqT0", [128, 22 * 512])
    mkdbg("wdq0", [128, D])
    mkdbg("xqT0", [128, 8 * 512])
    mkdbg("sil0", [128, 512])

    WCOLS = 46336

    from contextlib import ExitStack
    with ExitStack() as _es:
        def _sb(name, shape):
            return _es.enter_context(nc.sbuf_tensor(name, shape, F32)).ap()
        def _ps(name, shape):
            return _es.enter_context(nc.psum_tensor(name, shape, F32)).ap()
        def _sem(name):
            return _es.enter_context(nc.semaphore(name))
        S_sb = _sb("S_sb", [128, 4096])
        expCT = _sb("expCT", [128, NCH * 8])
        CcumT = _sb("CcumT", [128, 16])
        c_tril = _sb("c_tril", [128, 128])
        c_mask = _sb("c_mask", [128, 128])
        c_id = _sb("c_id", [128, 128])
        c_mk = _sb("c_mk", [128, 4])
        c_mki = _sb("c_mki", [128, 4])
        wgk2a_sb = _sb("wgk2a", [128, DK])
        wgk1_sb = _sb("wgk1s", [128, 8 * 32])
        rmsw_sb = _sb("rmsw_sb", [128, 16])
        stats = _sb("stats", [128, 64])
        blastT = _sb("blastT", [128, 8])
        eblT = _sb("eblT", [128, 8])
        cvals = _sb("cvals_sb", [128, 8])
        AR = _sb("AR", [128, WCOLS])
        pb2 = _ps("pb2", [128, 1024])
        pa = _ps("pa", [128, 512])
        po = _ps("po", [128, 2048])
        ps = _ps("ps", [128, 512])
        sems = {n: _sem("sem_" + n) for n in ("t", "v", "s", "g", "ld", "st")}
        block = _es.enter_context(nc.Block())
        b = Builder(block, sems)
        ar = Arena(AR, WCOLS)
        P = {}

        def pv(key):
            return P.get(key, (b.v, 0))

        b.load([], c_tril[:, :], ctrlneg[:, :])
        b.load([], c_mask[:, :], maskJT[:, :])
        b.load([], c_id[:, :], ident[:, :])
        b.load([], c_mk[:, :], mk_in[:, :])
        b.load([], c_mki[:, :], mki_in[:, :])
        b.load([], cvals[:, :], cv_in[:, :])
        b.load([], wgk2a_sb[:, :], wgk2Ta[:, :])
        b.load([], wgk1_sb.rearrange("p (dc r) -> p dc r", r=32),
               wgk1T.rearrange("(dc p) r -> p dc r", p=128))
        tok_consts = b.load([], rmsw_sb[:, :], rmsw2[:, :])
        b.boot = [tok_consts]
        tok_z = b.emit("v", [], lambda h: (h.memset(S_sb[:, :], 0.0),
                                           h.memset(CcumT[:, :], 0.0))[-1])

        def quantize_stage1(src, ncols, deps, scr):
            nb = ncols // 16
            bm, sc, rs = scr["bm"][:, 0:nb], scr["sc"][:, 0:nb], scr["rs"][:, 0:nb]
            src3 = src.rearrange("p (n k) -> p n k", k=16)
            tbm = b.emit("v", deps, lambda h: h.reduce_max(
                bm, src3, axis=AX.X, apply_absolute_value=True))
            tsc = b.emit("g", [tbm], lambda h: (
                h.tensor_scalar_mul(sc, bm, 1.0 / 6.0),
                h.tensor_scalar_max(sc, sc, 1e-12))[-1])
            return b.emit("v", [tsc], lambda h: h.reciprocal(rs, sc))

        def quantize_stage2(src, ncols, qout, deps, scr):
            nb = ncols // 16
            xn, u2 = scr["xn"][:, 0:ncols], scr["u2"][:, 0:ncols]
            L, H = scr["L"][:, 0:ncols], scr["H"][:, 0:ncols]
            sc, rs = scr["sc"][:, 0:nb], scr["rs"][:, 0:nb]
            src3 = src.rearrange("p (n k) -> p n k", k=16)
            rs3 = rs.unsqueeze(2).broadcast_to([128, nb, 16])
            xn3 = xn.rearrange("p (n k) -> p n k", k=16)
            b.emit("v", deps, lambda h: h.tensor_mul(xn3, src3, rs3))
            b.emit("v", [], lambda h: h.tensor_mul(u2, xn, xn))
            b.emit("v", [], lambda h: h.tensor_single_scalar(L, u2, 0.0625, ALU.is_gt))
            for m2 in (0.5625, 1.5625, 3.0625):
                b.emit("v", [], lambda h, m2=m2: h.scalar_tensor_tensor(
                    L, u2, m2, L, ALU.is_gt, ALU.add))
            b.emit("v", [], lambda h: h.tensor_single_scalar(H, u2, 25.0, ALU.is_gt))
            for m2 in (25.0, 6.25, 12.25):
                b.emit("v", [], lambda h, m2=m2: h.scalar_tensor_tensor(
                    H, u2, m2, H, ALU.is_gt, ALU.add))
            b.emit("v", [], lambda h: h.scalar_tensor_tensor(
                L, L, 0.5, H, ALU.mult, ALU.add))
            b.emit("v", [], lambda h: h.tensor_scalar(u2, xn, 0.0, 2.0,
                                                      ALU.is_gt, ALU.mult))
            b.emit("v", [], lambda h: h.tensor_single_scalar(u2, u2, 1.0, ALU.subtract))
            b.emit("v", [], lambda h: h.tensor_mul(H, L, u2))
            q3 = qout.rearrange("p (n k) -> p n k", k=16)
            H3 = H.rearrange("p (n k) -> p n k", k=16)
            sc3 = sc.unsqueeze(2).broadcast_to([128, nb, 16])
            return b.emit("v", [], lambda h: h.tensor_mul(q3, H3, sc3))

        def quantize_rows(src, ncols, qout, deps, scr):
            t1 = quantize_stage1(src, ncols, deps, scr)
            return quantize_stage2(src, ncols, qout, [t1], scr)

        def tr_group(src_cols, nblk, pdst, deps):
            def f(h):
                last = None
                for j in range(nblk):
                    last = h.transpose(pdst[:, j * 128:(j + 1) * 128],
                                       src_cols[:, j * 128:(j + 1) * 128], c_id[:, :])
                return last
            return b.emit("t", deps, f)

        LNPAR = {"i": 0}
        LND = {"buf": None}

        def ln_norm(src, out, deps):
            B0 = 8 * (LNPAR["i"] % 2)
            LNPAR["i"] += 1
            st_ = stats[:, B0:B0 + 8]
            tv = b.emit("v", deps, lambda h: h.reduce_sum(
                st_[:, 0:1], src, axis=AX.X))
            ts = b.emit("s", deps + [P.get("lnstats", (b.v, 0))], lambda h: h.activation(
                out=LND["buf"], in_=src, func=AF.Square, accum_out=st_[:, 1:2]))
            ts2 = b.emit("g", [tv, ts], lambda h: (
                h.tensor_scalar_mul(st_[:, 2:3], st_[:, 0:1], 1.0 / D),
                h.tensor_mul(st_[:, 3:4], st_[:, 2:3], st_[:, 2:3]))[-1])
            tv = b.emit("v", [ts2], lambda h: h.scalar_tensor_tensor(
                st_[:, 4:5], st_[:, 1:2], 1.0 / D, st_[:, 3:4],
                ALU.mult, ALU.subtract))
            P["lnstats"] = tv
            ts = b.emit("s", [tv], lambda h: (
                h.activation(out=st_[:, 5:6], in_=st_[:, 4:5], func=AF.Ln,
                             bias=cvals[:, 0:1]),
                h.activation(out=st_[:, 6:7], in_=st_[:, 5:6], func=AF.Exp,
                             scale=cvals[:, 2:3]))[-1])
            return b.emit("v", [ts, P.get("xnch_free", (b.t, 0))],
                          lambda h: h.tensor_scalar(
                out, src, st_[:, 2:3], st_[:, 6:7], ALU.subtract, ALU.mult))

        ar.reset()

        tok_W = b.all_toks()

        ar.reset()
        xch = ar.alloc(D)
        lndump = ar.alloc(D)
        xnch = ar.alloc(D)
        xnT = ar.alloc(8 * A_TILE)
        gneg = ar.alloc(CPT * DK)
        q_sb = ar.alloc(8 * A_TILE)
        k_sb = ar.alloc(8 * A_TILE)
        v_sb = ar.alloc(CPT * DV)
        glr = ar.alloc(CPT * 32)
        glrT = ar.alloc(A_TILE)
        ebT_sb = ar.alloc(DK)
        enbT_sb = ar.alloc(DK)
        qeT_sb = ar.alloc(DK)
        kdT_sb = ar.alloc(DK)
        kd_t_sb = ar.alloc(DK)
        Ash = ar.alloc(NH * CH)
        obuf = ar.alloc(DV // 2)
        gbuf = ar.alloc(512)
        tmpa = lndump
        scrA = {"xn": ar.alloc(512), "u2": ar.alloc(512), "L": ar.alloc(512),
                "H": ar.alloc(512), "bm": ar.alloc(32), "sc": ar.alloc(32),
                "rs": ar.alloc(32)}
        wrowA = scrA["u2"]
        wqA = scrA["L"]
        evacA = ar.alloc(512)
        wAB = ar.alloc(8192)
        wbufA = wAB[:, 0:4096]
        wbufB = wAB[:, 4096:8192]
        sdel = scrA["H"]

        LND["buf"] = lndump
        tok_ones = b.emit("v", tok_W, lambda h: h.memset(glrT[32:33, :], 1.0))

        WUN = []
        for m in range(2):
            for ic in range(N_IC):
                for off in (0, 512):
                    WUN.append((m, ic, off, 512))
        for fc in range(8):
            for off in range(0, I_FFN, 512):
                WUN.append((2, fc, off, min(512, I_FFN - off)))
        WPOSA = {"i": 0}

        def emit_wunitA():
            if WPOSA["i"] >= len(WUN):
                return
            m, ic, off, w = WUN[WPOSA["i"]]
            WPOSA["i"] += 1
            nat = (wg_nat, wu_nat, wd_nat)[m]
            nb = w // 128
            tl = b.load([pv("wrowA_free")], wrowA[:, 0:w],
                        nat[ic * 128:(ic + 1) * 128, off:off + w])
            t1 = quantize_stage1(wrowA[:, 0:w], w, [tl], scrA)
            tq = quantize_stage2(wrowA[:, 0:w], w, wqA[:, 0:w],
                                 [t1, pv("wqA_free")], scrA)
            P["wrowA_free"] = tq
            tt = b.emit("t", [tq, pv("paW")], lambda h, nb=nb: [
                h.transpose(pa[:, j * 128:(j + 1) * 128],
                            wqA[:, j * 128:(j + 1) * 128], c_id[:, :])
                for j in range(nb)][-1])
            P["wqA_free"] = tt
            te = b.emit("s", [tt, pv("evacA_free")], lambda h, nb=nb: h.copy(
                out=evacA[:, 0:nb * 128], in_=pa[:, 0:nb * 128]))
            P["paW"] = te
            if m < 2:
                scr_out = wgq_s if m == 0 else wuq_s
                piece, iblk = ic // 4, ic % 4
                fw_p = 512 if piece < 5 else 256
                base = (4 * (off // 512))
                dst = scr_out[piece, :, :].rearrange(
                    "p (dc f) -> p dc f", f=fw_p)[
                    :, base:base + nb, iblk * 128:(iblk + 1) * 128]
                tsw = b.store([te], dst,
                              evacA[:, 0:nb * 128].rearrange("p (j i) -> p j i", i=128))
            else:
                dst = wdq_s[off:off + w, ic * 128:(ic + 1) * 128].rearrange(
                    "(j p) i -> p j i", p=128)
                tsw = b.store([te], dst,
                              evacA[:, 0:nb * 128].rearrange("p (j i) -> p j i", i=128))
            P["evacA_free"] = tsw
        xnT3 = xnT.rearrange("p (dc t) -> p dc t", t=A_TILE)

        cg = 0
        for tile in range(N_AT):
            t0 = tile * A_TILE
            tok_xnT, tok_gneg = [], []
            for tc in range(CPT):
                tl = b.load([pv("xch")], xch, x_in[t0 + tc * CH:t0 + (tc + 1) * CH, :])
                tv = ln_norm(xch, xnch, [tl])
                P["xch"] = tv
                if debug and tile == 0 and tc == 0:
                    P["xch"] = b.store([tl, tv], dbg["xch0"][:, :], xch)
                    P["xnch_free"] = b.store([tv], dbg["xnch0"][:, :], xnch)
                    P["lnstats"] = b.store([tv], dbg["lnst0"][:, :], stats[:, 0:8])
                tt = tr_group(xnch, 8, pb2, [tv, pv("pb2")])
                P["xnch_free"] = tt
                te = b.emit("v", [tt], lambda h, tc=tc: h.tensor_copy(
                    xnT3[:, :, tc * CH:(tc + 1) * CH],
                    pb2.rearrange("p (dc i) -> p dc i", i=128)))
                P["pb2"] = te
                tok_xnT.append(te)
                tt = b.emit("t", [te, tok_consts, pv("pa")], lambda h, tc=tc: [
                    h.matmul(pa[:, 0:32], xnT3[:, dc, tc * CH:(tc + 1) * CH],
                             wgk1_sb[:, dc * 32:(dc + 1) * 32],
                             start=(dc == 0), stop=(dc == 7))
                    for dc in range(8)][-1])
                te2 = b.emit("v", [tt], lambda h, tc=tc: h.tensor_copy(
                    glr[:, tc * 32:(tc + 1) * 32], pa[:, 0:32]))
                tt = b.emit("t", [te2], lambda h, tc=tc: h.transpose(
                    pa[0:32, 128:256], glr[:, tc * 32:(tc + 1) * 32], c_id[:, :]))
                te2 = b.emit("v", [tt], lambda h, tc=tc: h.tensor_copy(
                    glrT[0:32, tc * CH:(tc + 1) * CH], pa[0:32, 128:256]))
                P["pa"] = te2
                tt = b.emit("t", [te2, tok_ones, pv("pb2")], lambda h, tc=tc: (
                    h.matmul(pb2[:, 0:512], glrT[0:33, tc * CH:(tc + 1) * CH],
                             wgk2a_sb[0:33, 0:512], start=True, stop=True),
                    h.matmul(pb2[:, 512:1024], glrT[0:33, tc * CH:(tc + 1) * CH],
                             wgk2a_sb[0:33, 512:1024], start=True, stop=True))[-1])
                te3 = b.emit("s", [tt, pv("gneg_free")], lambda h, tc=tc: (
                    h.activation(out=tmpa, in_=pb2[:, :], func=AF.Exp, scale=-1.0),
                    h.activation(out=gneg[:, tc * DK:(tc + 1) * DK], in_=tmpa,
                                 func=AF.Ln, bias=1.0))[-1])
                P["pb2"] = te3
                tok_gneg.append(te3)

            for (wsrc, dst, nm) in ((wqT, q_sb, "q"), (wkT, k_sb, "k")):
                for half in range(2):
                    wb = wbufA if half == 0 else wbufB
                    tlw = b.load([pv(f"wb{half}")], wb, wsrc[half, :, :])
                    wb3 = wb.rearrange("p (dc f) -> p dc f", f=512)
                    for fc in range(4):
                        fg = half * 4 + fc
                        pdst, pk = (pa, "pa") if fg % 2 == 0 else (ps, "ps")
                        tt = b.emit("t", tok_xnT + [tlw, pv(pk)],
                                    lambda h, wb3=wb3, fc=fc, pdst=pdst: [
                            h.matmul(pdst[:, 0:A_TILE],
                                     wb3[:, dc, fc * 128:(fc + 1) * 128],
                                     xnT3[:, dc, :], start=(dc == 0), stop=(dc == 7))
                            for dc in range(8)][-1])
                        te = b.emit("v", [tt, pv(f"{nm}_free")],
                                    lambda h, dst=dst, fg=fg, pdst=pdst: h.tensor_copy(
                            dst[:, fg * A_TILE:(fg + 1) * A_TILE],
                            pdst[:, 0:A_TILE]))
                        P[pk] = te
                    P[f"wb{half}"] = tt
                P[f"{nm}done"] = te

            for eq in range(4):
                wb = wbufA if eq % 2 == 0 else wbufB
                tlw = b.load([pv(f"wb{eq%2}")], wb, wvT[eq, :, :])
                wb3 = wb.rearrange("p (dc f) -> p dc f", f=512)
                for tc in range(CPT):
                    i = eq * CPT + tc
                    pdst, pk = (pa, "pa") if i % 2 == 0 else (ps, "ps")
                    tt = b.emit("t", tok_xnT + [tlw, pv(pk), pv("v_free")],
                                lambda h, wb3=wb3, tc=tc, pdst=pdst: [
                        h.matmul(pdst[:, 0:512], xnT3[:, dc, tc * CH:(tc + 1) * CH],
                                 wb3[:, dc, :], start=(dc == 0), stop=(dc == 7))
                        for dc in range(8)][-1])
                    te = b.emit("v", [tt], lambda h, tc=tc, eq=eq, pdst=pdst: h.tensor_copy(
                        v_sb[:, tc * DV + eq * 512:tc * DV + (eq + 1) * 512],
                        pdst[:, 0:512]))
                    P[pk] = te
                P[f"wb{eq%2}"] = tt
            tok_v = te

            for eq in range(4):
                wb = wbufA if eq % 2 == 0 else wbufB
                tlw = b.load([pv(f"wb{eq%2}")], wb, wgT[eq, :, :])
                wb3 = wb.rearrange("p (dc f) -> p dc f", f=512)
                for tc in range(CPT):
                    i = eq * CPT + tc
                    pdst, pk = (pa, "pa") if i % 2 == 0 else (ps, "ps")
                    tt = b.emit("t", tok_xnT + [tlw, pv(pk)],
                                lambda h, wb3=wb3, tc=tc, pdst=pdst: [
                        h.matmul(pdst[:, 0:512], xnT3[:, dc, tc * CH:(tc + 1) * CH],
                                 wb3[:, dc, :], start=(dc == 0), stop=(dc == 7))
                        for dc in range(8)][-1])
                    te = b.emit("s", [tt, pv("gbuf_free")],
                                lambda h, pdst=pdst: h.activation(
                        out=gbuf, in_=pdst[:, 0:512], func=AF.Silu))
                    P[pk] = te
                    tgs = b.store([te], gate_s[t0 + tc * CH:t0 + (tc + 1) * CH,
                                               eq * 512:(eq + 1) * 512], gbuf)
                    P["gbuf_free"] = tgs
                P[f"wb{eq%2}"] = tt

            for _ in range(34):
                emit_wunitA()

            for tc in range(CPT):
                par, parn = (cg % 2) * 8, ((cg + 1) % 2) * 8
                tt = b.emit("t", [tok_gneg[tc], tok_consts, pv("pb2")],
                            lambda h, tc=tc: [
                    h.matmul(pb2[:, g * 128:(g + 1) * 128],
                             gneg[:, tc * DK + g * 128:tc * DK + (g + 1) * 128],
                             c_tril[:, :], start=True, stop=True)
                    for g in range(8)][-1])
                ts = b.emit("s", [tt, pv("eb_free")], lambda h, par=par, cg=cg: (
                    h.activation(out=ebT_sb, in_=pb2[:, :], func=AF.Exp),
                    h.activation(out=enbT_sb, in_=pb2[:, :], func=AF.Exp, scale=-1.0),
                    h.activation(out=expCT[:, cg * 8:(cg + 1) * 8],
                                 in_=CcumT[:, par:par + 8], func=AF.Exp))[-1])
                tv = b.emit("v", [ts], lambda h: h.tensor_copy(
                    blastT[:, :],
                    pb2.rearrange("p (g i) -> p g i", i=128)[:, :, 127]))
                P["pb2"] = tv
                ts2 = b.emit("s", [tv], lambda h: h.activation(
                    out=eblT[:, :], in_=blastT[:, :], func=AF.Exp))
                qv = q_sb.rearrange("p (g t) -> p g t", t=A_TILE)[
                    :, :, tc * CH:(tc + 1) * CH]
                kv = k_sb.rearrange("p (g t) -> p g t", t=A_TILE)[
                    :, :, tc * CH:(tc + 1) * CH]
                eb3 = ebT_sb.rearrange("p (g i) -> p g i", i=128)
                en3 = enbT_sb.rearrange("p (g i) -> p g i", i=128)
                qe3 = qeT_sb.rearrange("p (g i) -> p g i", i=128)
                kd3 = kdT_sb.rearrange("p (g i) -> p g i", i=128)
                tqe = b.emit("v", [ts, P["qdone"], P["kdone"], pv("qeT_free"),
                                   pv("kdT_free")],
                             lambda h, par=par, parn=parn: (
                    h.scalar_tensor_tensor(qe3, qv, 1.0 / 16.0, eb3, ALU.mult, ALU.mult),
                    h.tensor_tensor(kd3, kv, en3, ALU.mult),
                    h.tensor_add(CcumT[:, parn:parn + 8], CcumT[:, par:par + 8],
                                 blastT[:, :]))[-1])
                P["eb_free"] = tqe
                tt = b.emit("t", [tqe, pv("pb2"), pv("kdt_free")], lambda h: [
                    h.transpose(pb2[:, g * 128:(g + 1) * 128],
                                kdT_sb[:, g * 128:(g + 1) * 128], c_id[:, :])
                    for g in range(8)][-1])
                tkd = b.emit("v", [tt], lambda h: h.tensor_copy(kd_t_sb, pb2[:, :]))
                P["pb2"] = tkd
                tt = b.emit("t", [tqe, pv("pa"), pv("ps"), pv("Ash_free")],
                            lambda h: [
                    h.matmul(pa[:, hh * 128:(hh + 1) * 128],
                             kdT_sb[:, (2 * hh + dc) * 128:(2 * hh + dc + 1) * 128],
                             qeT_sb[:, (2 * hh + dc) * 128:(2 * hh + dc + 1) * 128],
                             start=(dc == 0), stop=(dc == 1))
                    for hh in range(NH) for dc in range(2)][-1])
                tAs = b.emit("v", [tt], lambda h: h.tensor_mul(
                    Ash.rearrange("p (hh i) -> p hh i", i=128),
                    pa.rearrange("p (hh i) -> p hh i", i=128),
                    c_mask[:, :].unsqueeze(1).broadcast_to([128, NH, 128])))
                P["pa"] = tAs
                tom = b.emit("t", [tAs, tok_v, P.get("S", tok_z), pv("po")],
                             lambda h, tc=tc: [
                    mm for hh in range(NH) for mm in (
                        h.matmul(po[:, hh * 512:(hh + 1) * 512],
                                 qeT_sb[:, (2 * hh) * 128:(2 * hh + 1) * 128],
                                 S_sb[:, (2 * hh) * 512:(2 * hh + 1) * 512],
                                 start=True, stop=False),
                        h.matmul(po[:, hh * 512:(hh + 1) * 512],
                                 qeT_sb[:, (2 * hh + 1) * 128:(2 * hh + 2) * 128],
                                 S_sb[:, (2 * hh + 1) * 512:(2 * hh + 2) * 512],
                                 start=False, stop=False),
                        h.matmul(po[:, hh * 512:(hh + 1) * 512],
                                 Ash[:, hh * 128:(hh + 1) * 128],
                                 v_sb[:, tc * DV + hh * 512:tc * DV + (hh + 1) * 512],
                                 start=False, stop=True))][-1])
                P["Ash_free"] = tom
                for hh in range(NH):
                    tt = b.emit("t", [tkd, pv("pSd")], lambda h, hh=hh, tc=tc: (
                        h.matmul(pa[:, 0:512],
                                 kd_t_sb[:, (2 * hh) * 128:(2 * hh + 1) * 128],
                                 v_sb[:, tc * DV + hh * 512:tc * DV + (hh + 1) * 512],
                                 start=True, stop=True),
                        h.matmul(ps[:, 0:512],
                                 kd_t_sb[:, (2 * hh + 1) * 128:(2 * hh + 2) * 128],
                                 v_sb[:, tc * DV + hh * 512:tc * DV + (hh + 1) * 512],
                                 start=True, stop=True))[-1])
                    tsu = b.emit("v", [tt, ts2], lambda h, hh=hh: (
                        h.tensor_add(sdel[:, 0:512],
                                     S_sb[:, (2 * hh) * 512:(2 * hh + 1) * 512],
                                     pa[:, 0:512]),
                        h.tensor_scalar_mul(S_sb[:, (2 * hh) * 512:(2 * hh + 1) * 512],
                                            sdel[:, 0:512], eblT[:, 2 * hh:2 * hh + 1]),
                        h.tensor_add(sdel[:, 0:512],
                                     S_sb[:, (2 * hh + 1) * 512:(2 * hh + 2) * 512],
                                     ps[:, 0:512]),
                        h.tensor_scalar_mul(
                            S_sb[:, (2 * hh + 1) * 512:(2 * hh + 2) * 512],
                            sdel[:, 0:512], eblT[:, 2 * hh + 1:2 * hh + 2]))[-1])
                    P["pSd"] = tsu
                P["S"] = P["pSd"]
                P["kdt_free"] = P["pSd"]
                P["pa"] = P["pSd"]
                P["ps"] = P["pSd"]
                te = b.emit("v", [tom, pv("obuf_free")],
                            lambda h: h.tensor_copy(obuf, po[:, 0:1024]))
                ts1o = b.store([te], oloc_s[cg * 128:(cg + 1) * 128, 0:1024], obuf)
                if debug and cg == 0:
                    ts1o = b.store([te, ts1o], dbg["o0"][:, 0:1024], obuf)
                te = b.emit("v", [tom, ts1o],
                            lambda h: h.tensor_copy(obuf, po[:, 1024:2048]))
                P["po"] = te
                tso = b.store([te], oloc_s[cg * 128:(cg + 1) * 128, 1024:2048], obuf)
                P["obuf_free"] = tso
                tsq = b.store([tom], qeT_s[cg * 128:(cg + 1) * 128, :], qeT_sb)
                P["qeT_free"] = tsq
                P["kdT_free"] = tt
                if debug and cg == 0:
                    P["obuf_free"] = b.store([te, tso], dbg["o0"][:, 1024:2048], obuf)
                    P["eb_free"] = b.store([ts, tqe], dbg["ebT0"][:, :], ebT_sb)
                cg += 1
            P["v_free"] = P["S"]
            P["gneg_free"] = P["pb2"]
            P["q_free"] = P["eb_free"]
            P["k_free"] = P["eb_free"]

            if debug and tile == 0:
                b.store([P["qdone"]], dbg["q0"][:, :], q_sb)
                b.store([tok_v], dbg["v0"][:, :], v_sb)
                b.store([tok_xnT[-1]], dbg["xnT0"][:, :], xnT)
                b.store([tok_gneg[-1]], dbg["gneg0"][:, :], gneg)
                b.store([P["S"]], dbg["S1"][:, :], S_sb)

        if debug:
            tl = b.load([pv("evacA_free"), pv("wb0"), pv("wb1")],
                        xnT[:, 0:2816], wgq_s[0, :, 0:2816])
            b.store([tl], dbg["wgq0"][:, :], xnT[:, 0:2816])
            tl2 = b.load([tl], xnT[:, 0:D], wdq_s[0:128, :])
            b.store([tl2], dbg["wdq0"][:, :], xnT[:, 0:D])

        parF = (NCH % 2) * 8
        tsA = b.emit("s", [P["S"]], lambda h: h.activation(
            out=stats[:, 32:40], in_=CcumT[:, parF:parF + 8], func=AF.Exp))
        tg1 = b.store([P["S"]], ag_in[:, 0:4096], S_sb)
        tg2 = b.store([tsA], ag_in[:, 4096:4104], stats[:, 32:40])
        if debug:
            b.store([P["S"]], dbg["U"][:, :], S_sb)
        tcc = b.emit("g", [tg1, tg2], lambda h: h.collective_compute(
            "AllGather", ALU.bypass,
            replica_groups=[[0, 1, 2, 3], [4, 5, 6, 7]],
            ins=[ag_in[:, :]], outs=[ag_out[:, :]]))
        U0 = v_sb[:, 0:4096]
        U1 = wAB[:, 0:4096]
        U2 = xnT
        tmpS4b = wAB[:, 4096:8192]
        barA = [tcc, P["S"], b.t.tok(), b.v.tok(), b.s.tok()]
        tu = [b.load(barA, U0, ag_out[0:128, 0:4096]),
              b.load([tcc], U1, ag_out[128:256, 0:4096]),
              b.load([tcc], U2, ag_out[256:384, 0:4096])]
        for k in range(3):
            tu.append(b.load([tcc], stats[:, 40 + 8 * k:48 + 8 * k],
                             ag_out[k * 128:(k + 1) * 128, 4096:4104]))

        tvS = b.emit("v", tu + [tg1], lambda h: h.tensor_scalar_mul(
            S_sb[:, :], U0, c_mk[:, 0:1]))
        sc_eff = stats[:, 56:64]
        for k in (1, 2):
            tvS = b.emit("v", [tvS], lambda h, k=k: h.scalar_tensor_tensor(
                sc_eff, stats[:, 40 + 8 * k:48 + 8 * k], c_mk[:, k:k + 1],
                c_mki[:, k:k + 1].broadcast_to([128, 8]), ALU.mult, ALU.add))
            tvS = b.emit("v", [tvS], lambda h: [
                h.tensor_scalar_mul(tmpS4b[:, g * 512:(g + 1) * 512],
                                    S_sb[:, g * 512:(g + 1) * 512], sc_eff[:, g:g + 1])
                for g in range(8)][-1])
            tvS = b.emit("v", [tvS], lambda h, k=k: h.scalar_tensor_tensor(
                S_sb[:, :], U1 if k == 1 else U2, c_mk[:, k:k + 1], tmpS4b,
                ALU.mult, ALU.add))
        if debug:
            b.store([tvS], dbg["Sinit"][:, :], S_sb)
        tok_Sinit = tvS

        ar.reset()
        oldb = ar.alloc(DV)
        ofull = ar.alloc(DV)
        h2b = ar.alloc(DV)
        gld = ar.alloc(DV)
        qeTb = ar.alloc(DK)
        qeTsc = ar.alloc(DK)
        h2T = ar.alloc(16 * B_TILE)
        wvout = ar.alloc(8 * D)
        xb = ar.alloc(BPT * D)
        x1b = ar.alloc(BPT * D)
        h2T3 = h2T.rearrange("p (ec t) -> p ec t", t=B_TILE)

        tok_B0 = [tok_Sinit] + b.all_toks()
        first = True
        for bt in range(N_BT):
            t0 = bt * B_TILE
            tok_h2T = []
            for tc in range(BPT):
                cg2 = bt * BPT + tc
                dep0 = (tok_B0 if first else []) + [pv("oldb_free")]
                first = False
                tl1 = b.load(dep0, oldb, oloc_s[cg2 * 128:(cg2 + 1) * 128, :])
                tl2 = b.load([pv("qeTb_free")], qeTb,
                             qeT_s[cg2 * 128:(cg2 + 1) * 128, :])
                tl3 = b.load([pv("gld_free")], gld,
                             gate_s[cg2 * 128:(cg2 + 1) * 128, :])
                tv = b.emit("v", [tl2], lambda h, cg2=cg2: h.tensor_mul(
                    qeTsc.rearrange("p (g i) -> p g i", i=128),
                    qeTb.rearrange("p (g i) -> p g i", i=128),
                    expCT[:, cg2 * 8:(cg2 + 1) * 8].unsqueeze(2).broadcast_to(
                        [128, 8, 128])))
                P["qeTb_free"] = tv
                tom = b.emit("t", [tv, tok_Sinit, pv("poB")], lambda h: [
                    h.matmul(po[:, hh * 512:(hh + 1) * 512],
                             qeTsc[:, (2 * hh + dc) * 128:(2 * hh + dc + 1) * 128],
                             S_sb[:, (2 * hh + dc) * 512:(2 * hh + dc + 1) * 512],
                             start=(dc == 0), stop=(dc == 1))
                    for hh in range(NH) for dc in range(2)][-1])
                tof = b.emit("v", [tom, tl1], lambda h: h.tensor_add(
                    ofull, oldb, po[:, :]))
                P["poB"] = tof
                P["oldb_free"] = tof
                ts = b.emit("s", [tof], lambda h: [
                    h.activation(out=qeTb[:, 0:512],
                                 in_=ofull[hh * 512:(hh + 1) * 512] if False else
                                 ofull[:, hh * 512:(hh + 1) * 512],
                                 func=AF.Square, accum_out=stats[:, 48 + hh:49 + hh])
                    for hh in range(NH)][-1])
                ts = b.emit("s", [ts], lambda h: (
                    h.activation(out=stats[:, 52:56], in_=stats[:, 48:52],
                                 func=AF.Ln, scale=cvals[:, 1:2], bias=cvals[:, 0:1]),
                    h.activation(out=stats[:, 52:56], in_=stats[:, 52:56],
                                 func=AF.Exp, scale=cvals[:, 2:3]))[-1])
                th2 = b.emit("v", [ts, tl3, pv("h2b_free")], lambda h: [
                    h.scalar_tensor_tensor(h2b[:, hh * 512:(hh + 1) * 512],
                                           ofull[:, hh * 512:(hh + 1) * 512],
                                           stats[:, 52 + hh:53 + hh],
                                           gld[:, hh * 512:(hh + 1) * 512],
                                           ALU.mult, ALU.mult)
                    for hh in range(NH)][-1])
                P["gld_free"] = th2
                for g2 in range(2):
                    tt = tr_group(h2b[:, g2 * 1024:], 8, pb2, [th2, pv("pb2B")])
                    tvv = b.emit("v", [tt], lambda h, g2=g2, tc=tc: [
                        h.tensor_scalar_mul(
                            h2T3[:, g2 * 8 + j, tc * CH:(tc + 1) * CH],
                            pb2[:, j * 128:(j + 1) * 128],
                            rmsw_sb[:, g2 * 8 + j:g2 * 8 + j + 1])
                        for j in range(8)][-1])
                    P["pb2B"] = tvv
                tok_h2T.append(tvv)
                P["h2b_free"] = tvv
            tlx = b.load([pv("xb_free")], xb,
                         x_in[t0:t0 + B_TILE, :].rearrange("(q p) d -> p q d", p=128))
            banks = [po[:, 0:1024], po[:, 1024:2048], pb2, None]
            for half in range(2):
                tlw = b.load([pv("wvout_free")] + ([pv("poB"), pv("pb2B")] if half == 0 else []),
                             wvout, woutT[half, :, :])
                wv3 = wvout.rearrange("p (ec f) -> p ec f", f=D)
                for tc in range(BPT):
                    def f_wout(h, half=half, tc=tc, wv3=wv3):
                        last = None
                        for ec8 in range(8):
                            ecg = half * 8 + ec8
                            for fh in range(2):
                                if tc < 2:
                                    dstp = po[:, tc * 1024 + fh * 512:
                                              tc * 1024 + (fh + 1) * 512]
                                elif tc == 2:
                                    dstp = pb2[:, fh * 512:(fh + 1) * 512]
                                else:
                                    dstp = pa[:, 0:512] if fh == 0 else ps[:, 0:512]
                                last = h.matmul(
                                    dstp, h2T3[:, ecg, tc * CH:(tc + 1) * CH],
                                    wv3[:, ec8, fh * 512:(fh + 1) * 512],
                                    start=(half == 0 and ec8 == 0),
                                    stop=(half == 1 and ec8 == 7))
                        return last
                    deps = tok_h2T + [tlw] if half == 0 else [tlw]
                    if half == 0 and tc == 3:
                        deps = deps + [pv("pa"), pv("ps")]
                    tw = b.emit("t", deps, f_wout)
                P["wvout_free"] = tw
            tx1 = []
            for tc in range(BPT):
                if tc < 2:
                    srcp = po[:, tc * 1024:(tc + 1) * 1024]
                    tv = b.emit("v", [tw, tlx, pv("x1b_free")],
                                lambda h, tc=tc, srcp=srcp: h.tensor_add(
                        x1b[:, tc * D:(tc + 1) * D], xb[:, tc * D:(tc + 1) * D], srcp))
                elif tc == 2:
                    tv = b.emit("v", [tw, tlx, pv("x1b_free")],
                                lambda h, tc=tc: h.tensor_add(
                        x1b[:, tc * D:(tc + 1) * D], xb[:, tc * D:(tc + 1) * D],
                        pb2[:, :]))
                else:
                    tv = b.emit("v", [tw, tlx, pv("x1b_free")], lambda h, tc=tc: (
                        h.tensor_add(x1b[:, tc * D:tc * D + 512],
                                     xb[:, tc * D:tc * D + 512], pa[:, 0:512]),
                        h.tensor_add(x1b[:, tc * D + 512:(tc + 1) * D],
                                     xb[:, tc * D + 512:(tc + 1) * D], ps[:, 0:512]))[-1]
                    )
                tx1.append(tv)
            P["poB"] = tx1[1]
            P["pb2B"] = tx1[2]
            P["pa"] = tx1[3]
            P["ps"] = tx1[3]
            P["xb_free"] = tx1[-1]
            tsx = b.store(tx1, x1_s[t0:t0 + B_TILE, :].rearrange(
                              "(q p) d -> p q d", p=128),
                          x1b.rearrange("p (q d) -> p q d", d=D))
            P["x1_done"] = tsx
            P["x1b_free"] = tsx
            if debug and bt == 0:
                b.store(tok_h2T, dbg["h2T0"][:, :], h2T)
                b.store(tx1, dbg["x1c0"][:, :], x1b)

        tok_Bend = b.all_toks()

        ar.reset()
        x1f = ar.alloc(4 * D)
        xqsrc = ar.alloc(D)
        xqo = ar.alloc(D)
        xqT = ar.alloc(8 * F_TILE)
        hqTf = ar.alloc(N_IC * F_TILE // 4 * 4)
        wfA = ar.alloc(8 * 512)
        wfB = ar.alloc(8 * 512)
        silb = ar.alloc(1024)
        hb = ar.alloc(1024)
        hqb = ar.alloc(1024)
        wdq0 = ar.alloc(D)
        wdq1 = ar.alloc(D)
        outv = ar.alloc(D)
        scrF = {"xn": ar.alloc(D), "u2": ar.alloc(D), "L": ar.alloc(D),
                "H": ar.alloc(D), "bm": ar.alloc(64), "sc": ar.alloc(64),
                "rs": ar.alloc(64)}
        lndump2 = ar.alloc(D)
        LND["buf"] = lndump2
        P["poF"] = P.get("poB", (b.v, 0))
        xqT3 = xqT.rearrange("p (dc t) -> p dc t", t=F_TILE)
        hqT3 = hqTf.rearrange("p (ic t) -> p ic t", t=F_TILE)

        lnd = {"b": lndump2}

        first = True
        for ft in range(N_FT):
            t0 = ft * F_TILE
            tlx = b.load((tok_Bend if first else []) + [P.get("x1f_free", (b.v, 0)),
                                                        P["x1_done"]],
                         x1f, x1_s[t0:t0 + F_TILE, :].rearrange(
                             "(q p) d -> p q d", p=128))
            first = False
            tok_xqT = []
            for tc in range(4):
                srcc = x1f[:, tc * D:(tc + 1) * D]
                P["xnch_free"] = P.get("xqsrc_free", (b.v, 0))
                tv = ln_norm(srcc, xqsrc, [tlx])
                if debug and ft == 0 and tc == 0:
                    b.store([tlx], dbg["x1f0"][:, :], x1f[:, 0:D])
                tq = quantize_rows(xqsrc, D, xqo, [tv, pv("xqo_free")], scrF)
                P["xqsrc_free"] = tq
                if debug and ft == 0 and tc == 0:
                    P["xqsrc_free"] = b.store([tq], dbg["xn2c0"][:, :], xqsrc)
                tt = tr_group(xqo, 8, pb2, [tq, pv("pb2F")])
                P["xqo_free"] = tt
                te = b.emit("v", [tt], lambda h, tc=tc: h.tensor_copy(
                    xqT3[:, :, tc * CH:(tc + 1) * CH],
                    pb2.rearrange("p (dc i) -> p dc i", i=128)))
                P["pb2F"] = te
                tok_xqT.append(te)
                if debug and ft == 0 and tc == 0:
                    P["xqo_free"] = b.store([tq], dbg["xq0"][:, :], xqo)
            pieces = [(i * 512, 512) for i in range(5)] + [(2560, 256)]
            for (i0, fw) in pieces:
                piece = i0 // 512
                tlg = b.load([pv("wfA_free")], wfA[:, 0:8 * fw],
                             wgq_s[piece, :, 0:8 * fw])
                tlu = b.load([pv("wfB_free")], wfB[:, 0:8 * fw],
                             wuq_s[piece, :, 0:8 * fw])
                wg3 = wfA.rearrange("p (dc i) -> p dc i", i=512)
                wu3 = wfB.rearrange("p (dc i) -> p dc i", i=512)
                for tc in range(4):
                    def f_gu(h, tc=tc, fw=fw, i0=i0):
                        last = None
                        for dc in range(8):
                            last = h.matmul(pb2[:, 0:fw],
                                            xqT3[:, dc, tc * CH:(tc + 1) * CH],
                                            wfA[:, dc * fw:(dc + 1) * fw] if False else
                                            wfA.rearrange("p (dc i) -> p dc i", i=fw)[:, dc, :],
                                            start=(dc == 0), stop=(dc == 7))
                        for dc in range(8):
                            last = h.matmul(pb2[:, 512:512 + fw],
                                            xqT3[:, dc, tc * CH:(tc + 1) * CH],
                                            wfB.rearrange("p (dc i) -> p dc i", i=fw)[:, dc, :],
                                            start=(dc == 0), stop=(dc == 7))
                        return last
                    tt = b.emit("t", tok_xqT + [tlg, tlu, pv("pb2F")], f_gu)
                    tsl = b.emit("s", [tt, pv("silb_free")], lambda h, fw=fw: h.activation(
                        out=silb[:, 0:fw], in_=pb2[:, 0:fw], func=AF.Silu))
                    tv = b.emit("v", [tsl, pv("hb_free")], lambda h, fw=fw: h.tensor_mul(
                        hb[:, 0:fw], pb2[:, 512:512 + fw], silb[:, 0:fw]))
                    P["pb2F"] = tv
                    P["silb_free"] = tv
                    if debug and ft == 0 and i0 == 0 and tc == 0:
                        P["hb_free2"] = b.store([tv], dbg["hb0"][:, :], hb[:, 0:512])
                        P["silb_free"] = b.store([tv], dbg["sil0"][:, :], silb[:, 0:512])
                    tq = quantize_rows(hb[:, 0:fw], fw, hqb[:, 0:fw],
                                       [tv, pv("hqb_free"), pv("hb_free2")], scrF)
                    P["hb_free"] = tq
                    nbl = fw // 128
                    tt = tr_group(hqb, nbl, pa, [tq, pv("paF")])
                    P["hqb_free"] = tt
                    te = b.emit("v", [tt], lambda h, i0=i0, tc=tc, nbl=nbl: h.tensor_copy(
                        hqT3[:, i0 // 128:i0 // 128 + nbl, tc * CH:(tc + 1) * CH],
                        pa.rearrange("p (j i) -> p j i", i=128)[:, 0:nbl, :]))
                    P["paF"] = te
                    if debug and ft == 0 and i0 == 0 and tc == 0:
                        P["hqb_free"] = b.store([tq], dbg["hq0"][:, :], hqb[:, 0:512])
                P["wfA_free"] = tt
                P["wfB_free"] = tt
            tok_hqT = te
            if debug and ft == 0:
                b.store([te], dbg["hqT0"][:, :], hqTf)
                b.store([tok_xqT[-1]], dbg["xqT0"][:, :], xqT)
            tdown = None
            for ic in range(N_IC):
                wdq = wdq0 if ic % 2 == 0 else wdq1
                tlw = b.load([pv(f"wdq{ic%2}_free")], wdq, wdq_s[ic * 128:(ic + 1) * 128, :])
                deps = [tlw, tok_hqT]
                if ic == 0:
                    deps += [pv("pb2F"), pv("poF"), pv("paF"), pv("poB"), pv("ps")]
                def f_dn(h, ic=ic, wdq=wdq):
                    last = None
                    for tc in range(4):
                        for fh in range(2):
                            if tc < 2:
                                dstp = po[:, tc * 1024 + fh * 512:
                                          tc * 1024 + (fh + 1) * 512]
                            elif tc == 2:
                                dstp = pb2[:, fh * 512:(fh + 1) * 512]
                            else:
                                dstp = pa[:, 0:512] if fh == 0 else ps[:, 0:512]
                            last = h.matmul(dstp,
                                            hqT3[:, ic, tc * CH:(tc + 1) * CH],
                                            wdq[:, fh * 512:(fh + 1) * 512],
                                            start=(ic == 0), stop=(ic == N_IC - 1))
                    return last
                tdown = b.emit("t", deps, f_dn)
                P[f"wdq{ic%2}_free"] = tdown
            for tc in range(4):
                if tc < 2:
                    tv = b.emit("v", [tdown, pv("outv_free")], lambda h, tc=tc: h.tensor_add(
                        outv, x1f[:, tc * D:(tc + 1) * D], po[:, tc * 1024:(tc + 1) * 1024]))
                elif tc == 2:
                    tv = b.emit("v", [tdown, pv("outv_free")], lambda h, tc=tc: h.tensor_add(
                        outv, x1f[:, tc * D:(tc + 1) * D], pb2[:, :]))
                else:
                    tv = b.emit("v", [tdown, pv("outv_free")], lambda h, tc=tc: (
                        h.tensor_add(outv[:, 0:512], x1f[:, tc * D:tc * D + 512],
                                     pa[:, 0:512]),
                        h.tensor_add(outv[:, 512:1024], x1f[:, tc * D + 512:(tc + 1) * D],
                                     ps[:, 0:512]))[-1])
                tso = b.store([tv], y_out[t0 + tc * CH:t0 + (tc + 1) * CH, :], outv)
                P["outv_free"] = tso
            P["x1f_free"] = tv
            P["poB"] = tv
            P["poF"] = tv
            P["pb2F"] = tv
            P["paF"] = tv
            P["ps"] = tv

        b.final_wait()
    return nc, dbg


def _consts():
    j = np.arange(CH)[:, None]
    t = np.arange(CH)[None, :]
    ctrl = np.where(j <= t, np.float32(-1.0 / 32.0), np.float32(0.0)).astype(np.float32)
    mask = (j <= t).astype(np.float32)
    idm = np.eye(128, dtype=np.float32)
    return ctrl, mask, idm


def swz(wT, fw):
    Din, Dout = wT.shape
    npc, ndc = Dout // fw, Din // 128
    out = np.empty((npc, 128, ndc * fw), np.float32)
    w3 = wT.reshape(ndc, 128, Dout)
    for i in range(npc):
        out[i] = w3[:, :, i * fw:(i + 1) * fw].transpose(1, 0, 2).reshape(128, ndc * fw)
    return np.ascontiguousarray(out)


def swz_rows(wT, npieces):
    Din, Dout = wT.shape
    rp = Din // npieces
    ndc = rp // 128
    out = np.empty((npieces, 128, ndc * Dout), np.float32)
    for i in range(npieces):
        out[i] = wT[i * rp:(i + 1) * rp].reshape(ndc, 128, Dout).transpose(
            1, 0, 2).reshape(128, ndc * Dout)
    return np.ascontiguousarray(out)


def make_inputs(full, core):
    (x, Wq, Wk, Wv, Wg, Wgk1, Wgk2, bgk2, Wout, rms_w, w_gate, w_up, w_down) = full
    bsel, seg = core // 4, core % 4
    ctrl, mask, idm = _consts()
    C = np.ascontiguousarray
    wgk2Ta = np.zeros((128, DK), np.float32)
    wgk2Ta[0:32] = Wgk2.T
    wgk2Ta[32] = bgk2
    mk = np.zeros((128, 4), np.float32)
    for k in range(4):
        mk[:, k] = 1.0 if k < seg else 0.0
    return {
        "x": C(x[bsel, seg * T_CORE:(seg + 1) * T_CORE]),
        "wqT": swz(Wq.T, 512), "wkT": swz(Wk.T, 512),
        "wvT": swz(Wv.T, 512), "wgT": swz(Wg.T, 512),
        "wgk1T": C(Wgk1.T), "wgk2Ta": wgk2Ta, "woutT": swz_rows(Wout.T, 2),
        "rmsw2": np.ascontiguousarray(np.tile(rms_w, NH).astype(np.float32).reshape(16, 128).T),
        "wg_nat": C(w_gate), "wu_nat": C(w_up), "wd_nat": C(w_down),
        "ctrlneg": ctrl, "maskJT": mask, "ident": idm,
        "mk": mk, "mki": (1.0 - mk).astype(np.float32),
        "cvals": np.tile(np.array([EPS, 1.0 / 512.0, -0.5, 1.0 / 1024.0, 1e-12, -1e-12, 1.0 / 6.0, 0],
                                  np.float32), (128, 1)),
    }


_prog_cache = {}


def kernel(L0_attn_x, L0_attn_Wq, L0_attn_Wk, L0_attn_Wv, L0_attn_Wg,
           L0_attn_Wgk1, L0_attn_Wgk2, L0_attn_bgk2, L0_attn_Wout,
           L0_attn_rms_w, L0_ffn_w_gate, L0_ffn_w_up, L0_ffn_w_down):
    full = tuple(np.asarray(a, dtype=np.float32) for a in (
        L0_attn_x, L0_attn_Wq, L0_attn_Wk, L0_attn_Wv, L0_attn_Wg, L0_attn_Wgk1,
        L0_attn_Wgk2, L0_attn_bgk2, L0_attn_Wout, L0_attn_rms_w,
        L0_ffn_w_gate, L0_ffn_w_up, L0_ffn_w_down))
    if "nc" not in _prog_cache:
        _prog_cache["nc"], _ = build_program(False)
    nc = _prog_cache["nc"]
    in_maps = [make_inputs(full, c) for c in range(8)]
    res = run_bass_kernel_spmd(nc, in_maps, core_ids=list(range(8)))
    out = np.empty((2, 8192, D), np.float32)
    for c in range(8):
        bsel, seg = c // 4, c % 4
        out[bsel, seg * T_CORE:(seg + 1) * T_CORE] = res.results[c]["y"]
    return out


if __name__ == "__main__":
    nc, _ = build_program(False)
    print("built OK")


# version 11
# speedup vs baseline: 1.0828x; 1.0828x over previous
import numpy as np
import concourse.bass as bass
import concourse.mybir as mybir
from concourse.bass_utils import run_bass_kernel_spmd

F32 = mybir.dt.float32
AF = mybir.ActivationFunctionType
ALU = mybir.AluOpType
AX = mybir.AxisListType

D = 1024
DK = 1024
DV = 2048
NH = 4
T_CORE = 2048
CH = 128
NCH = T_CORE // CH
A_TILE = 512
N_AT = T_CORE // A_TILE
CPT = A_TILE // CH
B_TILE = 512
N_BT = T_CORE // B_TILE
BPT = B_TILE // CH
F_TILE = 512
N_FT = T_CORE // F_TILE
I_FFN = 2816
N_IC = I_FFN // 128
EPS = 1e-5


class Eng:
    def __init__(self, name, sem):
        self.name, self.sem, self.cnt = name, sem, 0

    def tok(self):
        return (self, self.cnt)


class Builder:
    def __init__(self, block, sems):
        self.t = Eng("t", sems["t"])
        self.v = Eng("v", sems["v"])
        self.s = Eng("s", sems["s"])
        self.g = Eng("g", sems["g"])
        self.ld = Eng("ld", sems["ld"])
        self.st = Eng("st", sems["st"])
        self.sec = {"t": block.tensor, "v": block.vector, "s": block.scalar,
                    "g": block.gpsimd, "y": block.sync}

    boot = None

    def on(self, eng, deps):
        e = getattr(self, eng)
        if self.boot is not None and not getattr(e, "booted", False):
            deps = list(self.boot) + list(deps)
            e.booted = True
        def deco(fn):
            def _(h):
                for (de, dc) in deps:
                    if dc > 0:
                        h.wait_ge(de.sem, dc)
                fn(h).then_inc(e.sem, 1)
            self.sec[eng](_)
            e.cnt += 1
            return e.tok()
        return deco

    def emit(self, eng, deps, fn):
        return self.on(eng, deps)(fn)

    def _dma(self, se, deps, out, in_):
        def _(h):
            for (de, dc) in deps:
                if dc > 0:
                    h.wait_ge(de.sem, dc)
            h.dma_start(out=out, in_=in_).then_inc(se.sem, 16)
        self.sec["y"](_)
        se.cnt += 16
        return se.tok()

    def load(self, deps, out, in_):
        return self._dma(self.ld, deps, out, in_)

    def store(self, deps, out, in_):
        return self._dma(self.st, deps, out, in_)

    def all_toks(self):
        return [self.t.tok(), self.v.tok(), self.s.tok(), self.g.tok(),
                self.ld.tok(), self.st.tok()]

    def final_wait(self):
        toks = self.all_toks()
        def _(h):
            for (de, dc) in toks:
                if dc > 0:
                    h.wait_ge(de.sem, dc)
        self.sec["y"](_)


class Arena:
    def __init__(self, sb, ncols):
        self.sb, self.ncols, self.off = sb, ncols, 0

    def alloc(self, cols):
        a = self.sb[:, self.off:self.off + cols]
        self.off += cols
        assert self.off <= self.ncols, f"arena overflow {self.off}>{self.ncols}"
        return a

    def reset(self):
        self.off = 0


def build_program(debug=False):
    nc = bass.Bass("TRN2", target_bir_lowering=False)
    dt = nc.dram_tensor

    x_in = dt("x", [T_CORE, D], F32, kind="ExternalInput").ap()
    wqT = dt("wqT", [2, 128, 8 * 512], F32, kind="ExternalInput").ap()
    wkT = dt("wkT", [2, 128, 8 * 512], F32, kind="ExternalInput").ap()
    wvT = dt("wvT", [4, 128, 8 * 512], F32, kind="ExternalInput").ap()
    wgT = dt("wgT", [4, 128, 8 * 512], F32, kind="ExternalInput").ap()
    wgk1T = dt("wgk1T", [D, 32], F32, kind="ExternalInput").ap()
    wgk2Ta = dt("wgk2Ta", [128, DK], F32, kind="ExternalInput").ap()
    woutT = dt("woutT", [2, 128, 8 * 1024], F32, kind="ExternalInput").ap()
    rmsw2 = dt("rmsw2", [128, 16], F32, kind="ExternalInput").ap()
    wg_nat = dt("wg_nat", [I_FFN, D], F32, kind="ExternalInput").ap()
    wu_nat = dt("wu_nat", [I_FFN, D], F32, kind="ExternalInput").ap()
    wd_nat = dt("wd_nat", [D, I_FFN], F32, kind="ExternalInput").ap()
    ctrlneg = dt("ctrlneg", [CH, CH], F32, kind="ExternalInput").ap()
    maskJT = dt("maskJT", [CH, CH], F32, kind="ExternalInput").ap()
    ident = dt("ident", [128, 128], F32, kind="ExternalInput").ap()
    mk_in = dt("mk", [128, 4], F32, kind="ExternalInput").ap()
    mki_in = dt("mki", [128, 4], F32, kind="ExternalInput").ap()
    cv_in = dt("cvals", [128, 8], F32, kind="ExternalInput").ap()

    y_out = dt("y", [T_CORE, D], F32, kind="ExternalOutput").ap()

    wgq_s = dt("wgq_s", [6, 128, 8 * 512], F32).ap()
    wuq_s = dt("wuq_s", [6, 128, 8 * 512], F32).ap()
    wdq_s = dt("wdq_s", [I_FFN, D], F32).ap()
    oloc_s = dt("oloc_s", [T_CORE, DV], F32).ap()
    qeT_s = dt("qeT_s", [NCH * 128, DK], F32).ap()
    gate_s = dt("gate_s", [T_CORE, DV], F32).ap()
    x1_s = dt("x1_s", [T_CORE, D], F32).ap()
    ag_in = dt("ag_in", [128, 4104], F32).ap()
    ag_out = dt("ag_out", [512, 4104], F32).ap()

    dbg = {}
    def mkdbg(name, shape):
        if debug:
            dbg[name] = dt("dbg_" + name, shape, F32, kind="ExternalOutput").ap()
    mkdbg("xch0", [128, D])
    mkdbg("xnch0", [128, D])
    mkdbg("lnst0", [128, 8])
    mkdbg("xnT0", [128, 8 * A_TILE])
    mkdbg("gneg0", [128, CPT * DK])
    mkdbg("q0", [128, 8 * A_TILE])
    mkdbg("v0", [128, CPT * DV])
    mkdbg("ebT0", [128, DK])
    mkdbg("o0", [128, DV])
    mkdbg("S1", [128, 4096])
    mkdbg("U", [128, 4096])
    mkdbg("Sinit", [128, 4096])
    mkdbg("wgq0", [128, I_FFN])
    mkdbg("h2T0", [128, 16 * 512])
    mkdbg("x1c0", [128, 4 * D])
    mkdbg("xq0", [128, D])
    mkdbg("x1f0", [128, D])
    mkdbg("xn2c0", [128, D])
    mkdbg("hq0", [128, 512])
    mkdbg("hb0", [128, 512])
    mkdbg("hqT0", [128, 22 * 512])
    mkdbg("wdq0", [128, D])
    mkdbg("xqT0", [128, 8 * 512])
    mkdbg("sil0", [128, 512])

    WCOLS = 46336

    from contextlib import ExitStack
    with ExitStack() as _es:
        def _sb(name, shape):
            return _es.enter_context(nc.sbuf_tensor(name, shape, F32)).ap()
        def _ps(name, shape):
            return _es.enter_context(nc.psum_tensor(name, shape, F32)).ap()
        def _sem(name):
            return _es.enter_context(nc.semaphore(name))
        S_sb = _sb("S_sb", [128, 4096])
        expCT = _sb("expCT", [128, NCH * 8])
        CcumT = _sb("CcumT", [128, 16])
        c_tril = _sb("c_tril", [128, 128])
        c_mask = _sb("c_mask", [128, 128])
        c_id = _sb("c_id", [128, 128])
        c_mk = _sb("c_mk", [128, 4])
        c_mki = _sb("c_mki", [128, 4])
        wgk2a_sb = _sb("wgk2a", [128, DK])
        wgk1_sb = _sb("wgk1s", [128, 8 * 32])
        rmsw_sb = _sb("rmsw_sb", [128, 16])
        stats = _sb("stats", [128, 64])
        blastT = _sb("blastT", [128, 8])
        eblT = _sb("eblT", [128, 8])
        cvals = _sb("cvals_sb", [128, 8])
        AR = _sb("AR", [128, WCOLS])
        pb2 = _ps("pb2", [128, 1024])
        pa = _ps("pa", [128, 512])
        po = _ps("po", [128, 2048])
        ps = _ps("ps", [128, 512])
        sems = {n: _sem("sem_" + n) for n in ("t", "v", "s", "g", "ld", "st")}
        block = _es.enter_context(nc.Block())
        b = Builder(block, sems)
        ar = Arena(AR, WCOLS)
        P = {}

        def pv(key):
            return P.get(key, (b.v, 0))

        b.load([], c_tril[:, :], ctrlneg[:, :])
        b.load([], c_mask[:, :], maskJT[:, :])
        b.load([], c_id[:, :], ident[:, :])
        b.load([], c_mk[:, :], mk_in[:, :])
        b.load([], c_mki[:, :], mki_in[:, :])
        b.load([], cvals[:, :], cv_in[:, :])
        b.load([], wgk2a_sb[:, :], wgk2Ta[:, :])
        b.load([], wgk1_sb.rearrange("p (dc r) -> p dc r", r=32),
               wgk1T.rearrange("(dc p) r -> p dc r", p=128))
        tok_consts = b.load([], rmsw_sb[:, :], rmsw2[:, :])
        b.boot = [tok_consts]
        tok_z = b.emit("v", [], lambda h: (h.memset(S_sb[:, :], 0.0),
                                           h.memset(CcumT[:, :], 0.0))[-1])

        def quantize_stage1(src, ncols, deps, scr):
            nb = ncols // 16
            bm, sc, rs = scr["bm"][:, 0:nb], scr["sc"][:, 0:nb], scr["rs"][:, 0:nb]
            src3 = src.rearrange("p (n k) -> p n k", k=16)
            tbm = b.emit("v", deps, lambda h: h.reduce_max(
                bm, src3, axis=AX.X, apply_absolute_value=True))
            tsc = b.emit("g", [tbm], lambda h: (
                h.tensor_scalar_mul(sc, bm, 1.0 / 6.0),
                h.tensor_scalar_max(sc, sc, 1e-12))[-1])
            return b.emit("v", [tsc], lambda h: h.reciprocal(rs, sc))

        def quantize_stage2(src, ncols, qout, deps, scr):
            nb = ncols // 16
            xn, u2 = scr["xn"][:, 0:ncols], scr["u2"][:, 0:ncols]
            L, H = scr["L"][:, 0:ncols], scr["H"][:, 0:ncols]
            sc, rs = scr["sc"][:, 0:nb], scr["rs"][:, 0:nb]
            src3 = src.rearrange("p (n k) -> p n k", k=16)
            rs3 = rs.unsqueeze(2).broadcast_to([128, nb, 16])
            xn3 = xn.rearrange("p (n k) -> p n k", k=16)
            b.emit("v", deps, lambda h: h.tensor_mul(xn3, src3, rs3))
            b.emit("v", [], lambda h: h.tensor_mul(u2, xn, xn))
            b.emit("v", [], lambda h: h.tensor_single_scalar(L, u2, 0.0625, ALU.is_gt))
            for m2 in (0.5625, 1.5625, 3.0625):
                b.emit("v", [], lambda h, m2=m2: h.scalar_tensor_tensor(
                    L, u2, m2, L, ALU.is_gt, ALU.add))
            b.emit("v", [], lambda h: h.tensor_single_scalar(H, u2, 25.0, ALU.is_gt))
            for m2 in (25.0, 6.25, 12.25):
                b.emit("v", [], lambda h, m2=m2: h.scalar_tensor_tensor(
                    H, u2, m2, H, ALU.is_gt, ALU.add))
            b.emit("v", [], lambda h: h.scalar_tensor_tensor(
                L, L, 0.5, H, ALU.mult, ALU.add))
            b.emit("v", [], lambda h: h.tensor_scalar(u2, xn, 0.0, 2.0,
                                                      ALU.is_gt, ALU.mult))
            b.emit("v", [], lambda h: h.tensor_single_scalar(u2, u2, 1.0, ALU.subtract))
            b.emit("v", [], lambda h: h.tensor_mul(H, L, u2))
            q3 = qout.rearrange("p (n k) -> p n k", k=16)
            H3 = H.rearrange("p (n k) -> p n k", k=16)
            sc3 = sc.unsqueeze(2).broadcast_to([128, nb, 16])
            return b.emit("v", [], lambda h: h.tensor_mul(q3, H3, sc3))

        def quantize_rows(src, ncols, qout, deps, scr):
            t1 = quantize_stage1(src, ncols, deps, scr)
            return quantize_stage2(src, ncols, qout, [t1], scr)

        def tr_group(src_cols, nblk, pdst, deps):
            def f(h):
                last = None
                for j in range(nblk):
                    last = h.transpose(pdst[:, j * 128:(j + 1) * 128],
                                       src_cols[:, j * 128:(j + 1) * 128], c_id[:, :])
                return last
            return b.emit("t", deps, f)

        LNPAR = {"i": 0}
        LND = {"buf": None}

        def ln_norm(src, out, deps):
            B0 = 8 * (LNPAR["i"] % 2)
            LNPAR["i"] += 1
            st_ = stats[:, B0:B0 + 8]
            tv = b.emit("v", deps, lambda h: h.reduce_sum(
                st_[:, 0:1], src, axis=AX.X))
            ts = b.emit("s", deps + [P.get("lnstats", (b.v, 0))], lambda h: h.activation(
                out=LND["buf"], in_=src, func=AF.Square, accum_out=st_[:, 1:2]))
            ts2 = b.emit("g", [tv, ts], lambda h: (
                h.tensor_scalar_mul(st_[:, 2:3], st_[:, 0:1], 1.0 / D),
                h.tensor_mul(st_[:, 3:4], st_[:, 2:3], st_[:, 2:3]))[-1])
            tv = b.emit("v", [ts2], lambda h: h.scalar_tensor_tensor(
                st_[:, 4:5], st_[:, 1:2], 1.0 / D, st_[:, 3:4],
                ALU.mult, ALU.subtract))
            P["lnstats"] = tv
            ts = b.emit("s", [tv], lambda h: (
                h.activation(out=st_[:, 5:6], in_=st_[:, 4:5], func=AF.Ln,
                             bias=cvals[:, 0:1]),
                h.activation(out=st_[:, 6:7], in_=st_[:, 5:6], func=AF.Exp,
                             scale=cvals[:, 2:3]))[-1])
            return b.emit("v", [ts, P.get("xnch_free", (b.t, 0))],
                          lambda h: h.tensor_scalar(
                out, src, st_[:, 2:3], st_[:, 6:7], ALU.subtract, ALU.mult))

        ar.reset()
        wrow = ar.alloc(I_FFN)
        wq_ = ar.alloc(I_FFN)
        scrW = {"xn": ar.alloc(I_FFN), "u2": ar.alloc(I_FFN),
                "L": ar.alloc(I_FFN), "H": ar.alloc(I_FFN),
                "bm": ar.alloc(176), "sc": ar.alloc(176), "rs": ar.alloc(176)}
        evacW = ar.alloc(1024)
        pbufW = ar.alloc(8 * 512)

        wqB_ = ar.alloc(I_FFN)
        wrowB = ar.alloc(I_FFN)
        scrWB = dict(scrW)
        scrWB["bm"] = ar.alloc(176)
        scrWB["sc"] = ar.alloc(176)
        scrWB["rs"] = ar.alloc(176)
        UNITS = [(wg_nat, wgq_s, ic, D, True) for ic in range(N_IC)] + \
                [(wu_nat, wuq_s, ic, D, True) for ic in range(N_IC)] + \
                [(wd_nat, wdq_s, ic, I_FFN, False) for ic in range(8)]
        slots = [wrow, wrowB]
        scrs = [scrW, scrWB]
        wqs = [wq_, wqB_]
        WTC = {"n": 0}
        EVQ = []
        pend = None
        for idx in range(len(UNITS) + 1):
            t1 = None
            if idx < len(UNITS):
                (nat, scr_out, ic, ncols, sw) = UNITS[idx]
                slot = idx % 2
                tl = b.load([pv(f"wrow{slot}_free")], slots[slot][:, 0:ncols],
                            nat[ic * 128:(ic + 1) * 128, :])
                t1 = quantize_stage1(slots[slot][:, 0:ncols], ncols, [tl], scrs[slot])
            if pend is not None:
                (natp, scrp, icp, ncolsp, swp), slotp, t1p = pend
                wqp = wqs[slotp]
                tq = quantize_stage2(slots[slotp][:, 0:ncolsp], ncolsp,
                                     wqp[:, 0:ncolsp], [t1p, pv(f"wq{slotp}_free")],
                                     scrs[slotp])
                P[f"wrow{slotp}_free"] = tq
                for fev in EVQ:
                    fev()
                EVQ.clear()
                nblk = ncolsp // 128
                if swp:
                    iblk = icp % 4
                    par = WTC["n"] % 2
                    WTC["n"] += 1
                    pps, key = (pb2, "pb2W") if par == 0 else (po, "poW")
                    tt = tr_group(wqp[:, 0:], 8, pps, [tq, pv(key)])
                    P[f"wq{slotp}_free"] = tt

                    def mk_ev(tt=tt, iblk=iblk, icp=icp, scrp=scrp,
                              pps=pps, key=key):
                        def fev():
                            te = b.emit("v", [tt, pv("pbufW")],
                                        lambda h: h.tensor_copy(
                                pbufW.rearrange("p (dc f) -> p dc f", f=512)[
                                    :, :, iblk * 128:(iblk + 1) * 128],
                                pps[:, 0:1024].rearrange(
                                    "p (dc i) -> p dc i", i=128)))
                            P[key] = te
                            if iblk == 3 or icp == N_IC - 1:
                                piece = icp // 4
                                if iblk == 3:
                                    tsw = b.store([te], scrp[piece, :, 0:4096],
                                                  pbufW)
                                else:
                                    tsw = b.store(
                                        [te], scrp[piece, :, 0:2048],
                                        pbufW.rearrange(
                                            "p (dc f) -> p dc f", f=512)[
                                            :, :, 0:256])
                                P["pbufW"] = tsw
                        return fev
                    EVQ.append(mk_ev())
                else:
                    for g0 in range(0, nblk, 8):
                        gn = min(8, nblk - g0)
                        par = WTC["n"] % 2
                        WTC["n"] += 1
                        pps, key = (pb2, "pb2W") if par == 0 else (po, "poW")
                        tt = tr_group(wqp[:, g0 * 128:], gn, pps, [tq, pv(key)])
                        te = b.emit("v", [tt, pv("evacW_free")],
                                    lambda h, gn=gn, pps=pps: h.tensor_copy(
                            evacW[:, 0:gn * 128], pps[:, 0:gn * 128]))
                        P[key] = te
                        dst = scrp[g0 * 128:(g0 + gn) * 128,
                                   icp * 128:(icp + 1) * 128].rearrange(
                            "(q p) i -> p q i", p=128)
                        P["evacW_free"] = b.store([te], dst,
                                                  evacW[:, 0:gn * 128].rearrange(
                            "p (q i) -> p q i", i=128))
                    P[f"wq{slotp}_free"] = tt
            pend = ((nat, scr_out, ic, ncols, sw), idx % 2, t1) if idx < len(UNITS) else None
        for fev in EVQ:
            fev()
        EVQ.clear()
        if debug:
            tl = b.load([b.st.tok()], wrow[:, 0:2816], wgq_s[0, :, 0:2816])
            b.store([tl], dbg["wgq0"][:, :], wrow[:, 0:2816])
            tl = b.load([b.st.tok()], wrow[:, 0:D], wdq_s[0:128, :])
            b.store([tl], dbg["wdq0"][:, :], wrow[:, 0:D])
        tok_W = b.all_toks()

        ar.reset()
        xch = ar.alloc(D)
        lndump = ar.alloc(D)
        xnch = ar.alloc(D)
        xnT = ar.alloc(8 * A_TILE)
        gneg = ar.alloc(CPT * DK)
        q_sb = ar.alloc(8 * A_TILE)
        k_sb = ar.alloc(8 * A_TILE)
        v_sb = ar.alloc(CPT * DV)
        glr = ar.alloc(CPT * 32)
        glrT = ar.alloc(A_TILE)
        ebT_sb = ar.alloc(DK)
        enbT_sb = ar.alloc(DK)
        qeT_sb = ar.alloc(DK)
        kdT_sb = ar.alloc(DK)
        kd_t_sb = ar.alloc(DK)
        Ash = ar.alloc(NH * CH)
        obuf = ar.alloc(DV)
        gbuf = ar.alloc(512)
        tmpa = lndump
        wAB = ar.alloc(8192)
        wbufA = wAB[:, 0:4096]
        wbufB = wAB[:, 4096:8192]
        sdel = ar.alloc(1024)

        LND["buf"] = lndump
        tok_ones = b.emit("v", tok_W, lambda h: h.memset(glrT[32:33, :], 1.0))
        xnT3 = xnT.rearrange("p (dc t) -> p dc t", t=A_TILE)

        cg = 0
        for tile in range(N_AT):
            t0 = tile * A_TILE
            tok_xnT, tok_gneg = [], []
            for tc in range(CPT):
                tl = b.load([pv("xch")], xch, x_in[t0 + tc * CH:t0 + (tc + 1) * CH, :])
                tv = ln_norm(xch, xnch, [tl])
                P["xch"] = tv
                if debug and tile == 0 and tc == 0:
                    P["xch"] = b.store([tl, tv], dbg["xch0"][:, :], xch)
                    P["xnch_free"] = b.store([tv], dbg["xnch0"][:, :], xnch)
                    P["lnstats"] = b.store([tv], dbg["lnst0"][:, :], stats[:, 0:8])
                tt = tr_group(xnch, 8, pb2, [tv, pv("pb2")])
                P["xnch_free"] = tt
                te = b.emit("v", [tt], lambda h, tc=tc: h.tensor_copy(
                    xnT3[:, :, tc * CH:(tc + 1) * CH],
                    pb2.rearrange("p (dc i) -> p dc i", i=128)))
                P["pb2"] = te
                tok_xnT.append(te)
                tt = b.emit("t", [te, tok_consts, pv("pa")], lambda h, tc=tc: [
                    h.matmul(pa[:, 0:32], xnT3[:, dc, tc * CH:(tc + 1) * CH],
                             wgk1_sb[:, dc * 32:(dc + 1) * 32],
                             start=(dc == 0), stop=(dc == 7))
                    for dc in range(8)][-1])
                te2 = b.emit("v", [tt], lambda h, tc=tc: h.tensor_copy(
                    glr[:, tc * 32:(tc + 1) * 32], pa[:, 0:32]))
                tt = b.emit("t", [te2], lambda h, tc=tc: h.transpose(
                    pa[0:32, 128:256], glr[:, tc * 32:(tc + 1) * 32], c_id[:, :]))
                te2 = b.emit("v", [tt], lambda h, tc=tc: h.tensor_copy(
                    glrT[0:32, tc * CH:(tc + 1) * CH], pa[0:32, 128:256]))
                P["pa"] = te2
                tt = b.emit("t", [te2, tok_ones, pv("pb2")], lambda h, tc=tc: (
                    h.matmul(pb2[:, 0:512], glrT[0:33, tc * CH:(tc + 1) * CH],
                             wgk2a_sb[0:33, 0:512], start=True, stop=True),
                    h.matmul(pb2[:, 512:1024], glrT[0:33, tc * CH:(tc + 1) * CH],
                             wgk2a_sb[0:33, 512:1024], start=True, stop=True))[-1])
                te3 = b.emit("s", [tt, pv("gneg_free")], lambda h, tc=tc: (
                    h.activation(out=tmpa, in_=pb2[:, :], func=AF.Exp, scale=-1.0),
                    h.activation(out=gneg[:, tc * DK:(tc + 1) * DK], in_=tmpa,
                                 func=AF.Ln, bias=1.0))[-1])
                P["pb2"] = te3
                tok_gneg.append(te3)

            for (wsrc, dst, nm) in ((wqT, q_sb, "q"), (wkT, k_sb, "k")):
                for half in range(2):
                    wb = wbufA if half == 0 else wbufB
                    tlw = b.load([pv(f"wb{half}")], wb, wsrc[half, :, :])
                    wb3 = wb.rearrange("p (dc f) -> p dc f", f=512)
                    for fc in range(4):
                        fg = half * 4 + fc
                        pdst, pk = (pa, "pa") if fg % 2 == 0 else (ps, "ps")
                        tt = b.emit("t", tok_xnT + [tlw, pv(pk)],
                                    lambda h, wb3=wb3, fc=fc, pdst=pdst: [
                            h.matmul(pdst[:, 0:A_TILE],
                                     wb3[:, dc, fc * 128:(fc + 1) * 128],
                                     xnT3[:, dc, :], start=(dc == 0), stop=(dc == 7))
                            for dc in range(8)][-1])
                        te = b.emit("v", [tt, pv(f"{nm}_free")],
                                    lambda h, dst=dst, fg=fg, pdst=pdst: h.tensor_copy(
                            dst[:, fg * A_TILE:(fg + 1) * A_TILE],
                            pdst[:, 0:A_TILE]))
                        P[pk] = te
                    P[f"wb{half}"] = tt
                P[f"{nm}done"] = te

            for eq in range(4):
                wb = wbufA if eq % 2 == 0 else wbufB
                tlw = b.load([pv(f"wb{eq%2}")], wb, wvT[eq, :, :])
                wb3 = wb.rearrange("p (dc f) -> p dc f", f=512)
                for tc in range(CPT):
                    i = eq * CPT + tc
                    pdst, pk = (pa, "pa") if i % 2 == 0 else (ps, "ps")
                    tt = b.emit("t", tok_xnT + [tlw, pv(pk), pv("v_free")],
                                lambda h, wb3=wb3, tc=tc, pdst=pdst: [
                        h.matmul(pdst[:, 0:512], xnT3[:, dc, tc * CH:(tc + 1) * CH],
                                 wb3[:, dc, :], start=(dc == 0), stop=(dc == 7))
                        for dc in range(8)][-1])
                    te = b.emit("v", [tt], lambda h, tc=tc, eq=eq, pdst=pdst: h.tensor_copy(
                        v_sb[:, tc * DV + eq * 512:tc * DV + (eq + 1) * 512],
                        pdst[:, 0:512]))
                    P[pk] = te
                P[f"wb{eq%2}"] = tt
            tok_v = te

            for tc in range(CPT):
                par, parn = (cg % 2) * 8, ((cg + 1) % 2) * 8
                tt = b.emit("t", [tok_gneg[tc], tok_consts, pv("pb2")],
                            lambda h, tc=tc: [
                    h.matmul(pb2[:, g * 128:(g + 1) * 128],
                             gneg[:, tc * DK + g * 128:tc * DK + (g + 1) * 128],
                             c_tril[:, :], start=True, stop=True)
                    for g in range(8)][-1])
                ts = b.emit("s", [tt, pv("eb_free")], lambda h, par=par, cg=cg: (
                    h.activation(out=ebT_sb, in_=pb2[:, :], func=AF.Exp),
                    h.activation(out=enbT_sb, in_=pb2[:, :], func=AF.Exp, scale=-1.0),
                    h.activation(out=expCT[:, cg * 8:(cg + 1) * 8],
                                 in_=CcumT[:, par:par + 8], func=AF.Exp))[-1])
                tv = b.emit("v", [ts], lambda h: h.tensor_copy(
                    blastT[:, :],
                    pb2.rearrange("p (g i) -> p g i", i=128)[:, :, 127]))
                P["pb2"] = tv
                ts2 = b.emit("s", [tv], lambda h: h.activation(
                    out=eblT[:, :], in_=blastT[:, :], func=AF.Exp))
                qv = q_sb.rearrange("p (g t) -> p g t", t=A_TILE)[
                    :, :, tc * CH:(tc + 1) * CH]
                kv = k_sb.rearrange("p (g t) -> p g t", t=A_TILE)[
                    :, :, tc * CH:(tc + 1) * CH]
                eb3 = ebT_sb.rearrange("p (g i) -> p g i", i=128)
                en3 = enbT_sb.rearrange("p (g i) -> p g i", i=128)
                qe3 = qeT_sb.rearrange("p (g i) -> p g i", i=128)
                kd3 = kdT_sb.rearrange("p (g i) -> p g i", i=128)
                tqe = b.emit("v", [ts, P["qdone"], P["kdone"], pv("qeT_free"),
                                   pv("kdT_free")],
                             lambda h, par=par, parn=parn: (
                    h.scalar_tensor_tensor(qe3, qv, 1.0 / 16.0, eb3, ALU.mult, ALU.mult),
                    h.tensor_tensor(kd3, kv, en3, ALU.mult),
                    h.tensor_add(CcumT[:, parn:parn + 8], CcumT[:, par:par + 8],
                                 blastT[:, :]))[-1])
                P["eb_free"] = tqe
                tt = b.emit("t", [tqe, pv("pb2"), pv("kdt_free")], lambda h: [
                    h.transpose(pb2[:, g * 128:(g + 1) * 128],
                                kdT_sb[:, g * 128:(g + 1) * 128], c_id[:, :])
                    for g in range(8)][-1])
                tkd = b.emit("v", [tt], lambda h: h.tensor_copy(kd_t_sb, pb2[:, :]))
                P["pb2"] = tkd
                tt = b.emit("t", [tqe, pv("pa"), pv("ps"), pv("Ash_free")],
                            lambda h: [
                    h.matmul(pa[:, hh * 128:(hh + 1) * 128],
                             kdT_sb[:, (2 * hh + dc) * 128:(2 * hh + dc + 1) * 128],
                             qeT_sb[:, (2 * hh + dc) * 128:(2 * hh + dc + 1) * 128],
                             start=(dc == 0), stop=(dc == 1))
                    for hh in range(NH) for dc in range(2)][-1])
                tAs = b.emit("v", [tt], lambda h: h.tensor_mul(
                    Ash.rearrange("p (hh i) -> p hh i", i=128),
                    pa.rearrange("p (hh i) -> p hh i", i=128),
                    c_mask[:, :].unsqueeze(1).broadcast_to([128, NH, 128])))
                P["pa"] = tAs
                tom = b.emit("t", [tAs, tok_v, P.get("S", tok_z), pv("po")],
                             lambda h, tc=tc: [
                    mm for hh in range(NH) for mm in (
                        h.matmul(po[:, hh * 512:(hh + 1) * 512],
                                 qeT_sb[:, (2 * hh) * 128:(2 * hh + 1) * 128],
                                 S_sb[:, (2 * hh) * 512:(2 * hh + 1) * 512],
                                 start=True, stop=False),
                        h.matmul(po[:, hh * 512:(hh + 1) * 512],
                                 qeT_sb[:, (2 * hh + 1) * 128:(2 * hh + 2) * 128],
                                 S_sb[:, (2 * hh + 1) * 512:(2 * hh + 2) * 512],
                                 start=False, stop=False),
                        h.matmul(po[:, hh * 512:(hh + 1) * 512],
                                 Ash[:, hh * 128:(hh + 1) * 128],
                                 v_sb[:, tc * DV + hh * 512:tc * DV + (hh + 1) * 512],
                                 start=False, stop=True))][-1])
                P["Ash_free"] = tom
                for hh in range(NH):
                    tt = b.emit("t", [tkd, pv("pSd")], lambda h, hh=hh, tc=tc: (
                        h.matmul(pa[:, 0:512],
                                 kd_t_sb[:, (2 * hh) * 128:(2 * hh + 1) * 128],
                                 v_sb[:, tc * DV + hh * 512:tc * DV + (hh + 1) * 512],
                                 start=True, stop=True),
                        h.matmul(ps[:, 0:512],
                                 kd_t_sb[:, (2 * hh + 1) * 128:(2 * hh + 2) * 128],
                                 v_sb[:, tc * DV + hh * 512:tc * DV + (hh + 1) * 512],
                                 start=True, stop=True))[-1])
                    tsu = b.emit("v", [tt, ts2], lambda h, hh=hh: (
                        h.tensor_add(sdel[:, 0:512],
                                     S_sb[:, (2 * hh) * 512:(2 * hh + 1) * 512],
                                     pa[:, 0:512]),
                        h.tensor_add(sdel[:, 512:1024],
                                     S_sb[:, (2 * hh + 1) * 512:(2 * hh + 2) * 512],
                                     ps[:, 0:512]),
                        h.tensor_scalar_mul(S_sb[:, (2 * hh) * 512:(2 * hh + 1) * 512],
                                            sdel[:, 0:512], eblT[:, 2 * hh:2 * hh + 1]),
                        h.tensor_scalar_mul(
                            S_sb[:, (2 * hh + 1) * 512:(2 * hh + 2) * 512],
                            sdel[:, 512:1024], eblT[:, 2 * hh + 1:2 * hh + 2]))[-1])
                    P["pSd"] = tsu
                P["S"] = P["pSd"]
                P["kdt_free"] = P["pSd"]
                P["pa"] = P["pSd"]
                P["ps"] = P["pSd"]
                te = b.emit("v", [tom, pv("obuf_free")],
                            lambda h: h.tensor_copy(obuf, po[:, :]))
                P["po"] = te
                tso = b.store([te], oloc_s[cg * 128:(cg + 1) * 128, :], obuf)
                P["obuf_free"] = tso
                tsq = b.store([tom], qeT_s[cg * 128:(cg + 1) * 128, :], qeT_sb)
                P["qeT_free"] = tsq
                P["kdT_free"] = tt
                if debug and cg == 0:
                    P["obuf_free"] = b.store([te], dbg["o0"][:, :], obuf)
                    P["eb_free"] = b.store([ts, tqe], dbg["ebT0"][:, :], ebT_sb)
                cg += 1
            P["v_free"] = P["S"]
            P["gneg_free"] = P["pb2"]
            P["q_free"] = P["eb_free"]
            P["k_free"] = P["eb_free"]

            for eq in range(4):
                wb = wbufA if eq % 2 == 0 else wbufB
                tlw = b.load([pv(f"wb{eq%2}")], wb, wgT[eq, :, :])
                wb3 = wb.rearrange("p (dc f) -> p dc f", f=512)
                for tc in range(CPT):
                    i = eq * CPT + tc
                    pdst, pk = (pa, "pa") if i % 2 == 0 else (ps, "ps")
                    tt = b.emit("t", tok_xnT + [tlw, pv(pk)],
                                lambda h, wb3=wb3, tc=tc, pdst=pdst: [
                        h.matmul(pdst[:, 0:512], xnT3[:, dc, tc * CH:(tc + 1) * CH],
                                 wb3[:, dc, :], start=(dc == 0), stop=(dc == 7))
                        for dc in range(8)][-1])
                    te = b.emit("s", [tt, pv("gbuf_free")],
                                lambda h, pdst=pdst: h.activation(
                        out=gbuf, in_=pdst[:, 0:512], func=AF.Silu))
                    P[pk] = te
                    tgs = b.store([te], gate_s[t0 + tc * CH:t0 + (tc + 1) * CH,
                                               eq * 512:(eq + 1) * 512], gbuf)
                    P["gbuf_free"] = tgs
                P[f"wb{eq%2}"] = tt

            if debug and tile == 0:
                b.store([P["qdone"]], dbg["q0"][:, :], q_sb)
                b.store([tok_v], dbg["v0"][:, :], v_sb)
                b.store([tok_xnT[-1]], dbg["xnT0"][:, :], xnT)
                b.store([tok_gneg[-1]], dbg["gneg0"][:, :], gneg)
                b.store([P["S"]], dbg["S1"][:, :], S_sb)

        parF = (NCH % 2) * 8
        tsA = b.emit("s", [P["S"]], lambda h: h.activation(
            out=stats[:, 32:40], in_=CcumT[:, parF:parF + 8], func=AF.Exp))
        tg1 = b.store([P["S"]], ag_in[:, 0:4096], S_sb)
        tg2 = b.store([tsA], ag_in[:, 4096:4104], stats[:, 32:40])
        if debug:
            b.store([P["S"]], dbg["U"][:, :], S_sb)
        tcc = b.emit("g", [tg1, tg2], lambda h: h.collective_compute(
            "AllGather", ALU.bypass,
            replica_groups=[[0, 1, 2, 3], [4, 5, 6, 7]],
            ins=[ag_in[:, :]], outs=[ag_out[:, :]]))
        U0 = v_sb[:, 0:4096]
        U1 = wAB[:, 0:4096]
        U2 = xnT
        tmpS4b = wAB[:, 4096:8192]
        barA = [tcc, P["S"], b.t.tok(), b.v.tok(), b.s.tok()]
        tu = [b.load(barA, U0, ag_out[0:128, 0:4096]),
              b.load([tcc], U1, ag_out[128:256, 0:4096]),
              b.load([tcc], U2, ag_out[256:384, 0:4096])]
        for k in range(3):
            tu.append(b.load([tcc], stats[:, 40 + 8 * k:48 + 8 * k],
                             ag_out[k * 128:(k + 1) * 128, 4096:4104]))

        tvS = b.emit("v", tu + [tg1], lambda h: h.tensor_scalar_mul(
            S_sb[:, :], U0, c_mk[:, 0:1]))
        sc_eff = stats[:, 56:64]
        for k in (1, 2):
            tvS = b.emit("v", [tvS], lambda h, k=k: h.scalar_tensor_tensor(
                sc_eff, stats[:, 40 + 8 * k:48 + 8 * k], c_mk[:, k:k + 1],
                c_mki[:, k:k + 1].broadcast_to([128, 8]), ALU.mult, ALU.add))
            tvS = b.emit("v", [tvS], lambda h: [
                h.tensor_scalar_mul(tmpS4b[:, g * 512:(g + 1) * 512],
                                    S_sb[:, g * 512:(g + 1) * 512], sc_eff[:, g:g + 1])
                for g in range(8)][-1])
            tvS = b.emit("v", [tvS], lambda h, k=k: h.scalar_tensor_tensor(
                S_sb[:, :], U1 if k == 1 else U2, c_mk[:, k:k + 1], tmpS4b,
                ALU.mult, ALU.add))
        if debug:
            b.store([tvS], dbg["Sinit"][:, :], S_sb)
        tok_Sinit = tvS

        ar.reset()
        oldb = ar.alloc(DV)
        ofull = ar.alloc(DV)
        h2b = ar.alloc(DV)
        gld = ar.alloc(DV)
        qeTb = ar.alloc(DK)
        qeTsc = ar.alloc(DK)
        h2T = ar.alloc(16 * B_TILE)
        wvout = ar.alloc(8 * D)
        xb = ar.alloc(BPT * D)
        x1b = ar.alloc(BPT * D)
        h2T3 = h2T.rearrange("p (ec t) -> p ec t", t=B_TILE)

        tok_B0 = [tok_Sinit] + b.all_toks()
        first = True
        for bt in range(N_BT):
            t0 = bt * B_TILE
            tok_h2T = []
            for tc in range(BPT):
                cg2 = bt * BPT + tc
                dep0 = (tok_B0 if first else []) + [pv("oldb_free")]
                first = False
                tl1 = b.load(dep0, oldb, oloc_s[cg2 * 128:(cg2 + 1) * 128, :])
                tl2 = b.load([pv("qeTb_free")], qeTb,
                             qeT_s[cg2 * 128:(cg2 + 1) * 128, :])
                tl3 = b.load([pv("gld_free")], gld,
                             gate_s[cg2 * 128:(cg2 + 1) * 128, :])
                tv = b.emit("v", [tl2], lambda h, cg2=cg2: h.tensor_mul(
                    qeTsc.rearrange("p (g i) -> p g i", i=128),
                    qeTb.rearrange("p (g i) -> p g i", i=128),
                    expCT[:, cg2 * 8:(cg2 + 1) * 8].unsqueeze(2).broadcast_to(
                        [128, 8, 128])))
                P["qeTb_free"] = tv
                tom = b.emit("t", [tv, tok_Sinit, pv("poB")], lambda h: [
                    h.matmul(po[:, hh * 512:(hh + 1) * 512],
                             qeTsc[:, (2 * hh + dc) * 128:(2 * hh + dc + 1) * 128],
                             S_sb[:, (2 * hh + dc) * 512:(2 * hh + dc + 1) * 512],
                             start=(dc == 0), stop=(dc == 1))
                    for hh in range(NH) for dc in range(2)][-1])
                tof = b.emit("v", [tom, tl1], lambda h: h.tensor_add(
                    ofull, oldb, po[:, :]))
                P["poB"] = tof
                P["oldb_free"] = tof
                ts = b.emit("s", [tof], lambda h: [
                    h.activation(out=qeTb[:, 0:512],
                                 in_=ofull[hh * 512:(hh + 1) * 512] if False else
                                 ofull[:, hh * 512:(hh + 1) * 512],
                                 func=AF.Square, accum_out=stats[:, 48 + hh:49 + hh])
                    for hh in range(NH)][-1])
                ts = b.emit("s", [ts], lambda h: (
                    h.activation(out=stats[:, 52:56], in_=stats[:, 48:52],
                                 func=AF.Ln, scale=cvals[:, 1:2], bias=cvals[:, 0:1]),
                    h.activation(out=stats[:, 52:56], in_=stats[:, 52:56],
                                 func=AF.Exp, scale=cvals[:, 2:3]))[-1])
                th2 = b.emit("v", [ts, tl3, pv("h2b_free")], lambda h: [
                    h.scalar_tensor_tensor(h2b[:, hh * 512:(hh + 1) * 512],
                                           ofull[:, hh * 512:(hh + 1) * 512],
                                           stats[:, 52 + hh:53 + hh],
                                           gld[:, hh * 512:(hh + 1) * 512],
                                           ALU.mult, ALU.mult)
                    for hh in range(NH)][-1])
                P["gld_free"] = th2
                for g2 in range(2):
                    tt = tr_group(h2b[:, g2 * 1024:], 8, pb2, [th2, pv("pb2B")])
                    tvv = b.emit("v", [tt], lambda h, g2=g2, tc=tc: [
                        h.tensor_scalar_mul(
                            h2T3[:, g2 * 8 + j, tc * CH:(tc + 1) * CH],
                            pb2[:, j * 128:(j + 1) * 128],
                            rmsw_sb[:, g2 * 8 + j:g2 * 8 + j + 1])
                        for j in range(8)][-1])
                    P["pb2B"] = tvv
                tok_h2T.append(tvv)
                P["h2b_free"] = tvv
            tlx = b.load([pv("xb_free")], xb,
                         x_in[t0:t0 + B_TILE, :].rearrange("(q p) d -> p q d", p=128))
            banks = [po[:, 0:1024], po[:, 1024:2048], pb2, None]
            for half in range(2):
                tlw = b.load([pv("wvout_free")] + ([pv("poB"), pv("pb2B")] if half == 0 else []),
                             wvout, woutT[half, :, :])
                wv3 = wvout.rearrange("p (ec f) -> p ec f", f=D)
                for tc in range(BPT):
                    def f_wout(h, half=half, tc=tc, wv3=wv3):
                        last = None
                        for ec8 in range(8):
                            ecg = half * 8 + ec8
                            for fh in range(2):
                                if tc < 2:
                                    dstp = po[:, tc * 1024 + fh * 512:
                                              tc * 1024 + (fh + 1) * 512]
                                elif tc == 2:
                                    dstp = pb2[:, fh * 512:(fh + 1) * 512]
                                else:
                                    dstp = pa[:, 0:512] if fh == 0 else ps[:, 0:512]
                                last = h.matmul(
                                    dstp, h2T3[:, ecg, tc * CH:(tc + 1) * CH],
                                    wv3[:, ec8, fh * 512:(fh + 1) * 512],
                                    start=(half == 0 and ec8 == 0),
                                    stop=(half == 1 and ec8 == 7))
                        return last
                    deps = tok_h2T + [tlw] if half == 0 else [tlw]
                    if half == 0 and tc == 3:
                        deps = deps + [pv("pa"), pv("ps")]
                    tw = b.emit("t", deps, f_wout)
                P["wvout_free"] = tw
            tx1 = []
            for tc in range(BPT):
                if tc < 2:
                    srcp = po[:, tc * 1024:(tc + 1) * 1024]
                    tv = b.emit("v", [tw, tlx, pv("x1b_free")],
                                lambda h, tc=tc, srcp=srcp: h.tensor_add(
                        x1b[:, tc * D:(tc + 1) * D], xb[:, tc * D:(tc + 1) * D], srcp))
                elif tc == 2:
                    tv = b.emit("v", [tw, tlx, pv("x1b_free")],
                                lambda h, tc=tc: h.tensor_add(
                        x1b[:, tc * D:(tc + 1) * D], xb[:, tc * D:(tc + 1) * D],
                        pb2[:, :]))
                else:
                    tv = b.emit("v", [tw, tlx, pv("x1b_free")], lambda h, tc=tc: (
                        h.tensor_add(x1b[:, tc * D:tc * D + 512],
                                     xb[:, tc * D:tc * D + 512], pa[:, 0:512]),
                        h.tensor_add(x1b[:, tc * D + 512:(tc + 1) * D],
                                     xb[:, tc * D + 512:(tc + 1) * D], ps[:, 0:512]))[-1]
                    )
                tx1.append(tv)
            P["poB"] = tx1[1]
            P["pb2B"] = tx1[2]
            P["pa"] = tx1[3]
            P["ps"] = tx1[3]
            P["xb_free"] = tx1[-1]
            tsx = b.store(tx1, x1_s[t0:t0 + B_TILE, :].rearrange(
                              "(q p) d -> p q d", p=128),
                          x1b.rearrange("p (q d) -> p q d", d=D))
            P["x1_done"] = tsx
            P["x1b_free"] = tsx
            if debug and bt == 0:
                b.store(tok_h2T, dbg["h2T0"][:, :], h2T)
                b.store(tx1, dbg["x1c0"][:, :], x1b)

        tok_Bend = b.all_toks()

        ar.reset()
        x1f = ar.alloc(4 * D)
        xqsrc = ar.alloc(D)
        xqo = ar.alloc(D)
        xqT = ar.alloc(8 * F_TILE)
        hqTf = ar.alloc(N_IC * F_TILE // 4 * 4)
        wfA = ar.alloc(8 * 512)
        wfB = ar.alloc(8 * 512)
        silb = ar.alloc(1024)
        hb = ar.alloc(1024)
        hqb = ar.alloc(1024)
        wdq0 = ar.alloc(D)
        wdq1 = ar.alloc(D)
        outv = ar.alloc(D)
        scrF = {"xn": ar.alloc(D), "u2": ar.alloc(D), "L": ar.alloc(D),
                "H": ar.alloc(D), "bm": ar.alloc(64), "sc": ar.alloc(64),
                "rs": ar.alloc(64)}
        lndump2 = ar.alloc(D)
        LND["buf"] = lndump2
        P["poF"] = P.get("poB", (b.v, 0))
        xqT3 = xqT.rearrange("p (dc t) -> p dc t", t=F_TILE)
        hqT3 = hqTf.rearrange("p (ic t) -> p ic t", t=F_TILE)

        lnd = {"b": lndump2}

        first = True
        for ft in range(N_FT):
            t0 = ft * F_TILE
            tlx = b.load((tok_Bend if first else []) + [P.get("x1f_free", (b.v, 0)),
                                                        P["x1_done"]],
                         x1f, x1_s[t0:t0 + F_TILE, :].rearrange(
                             "(q p) d -> p q d", p=128))
            first = False
            tok_xqT = []
            for tc in range(4):
                srcc = x1f[:, tc * D:(tc + 1) * D]
                P["xnch_free"] = P.get("xqsrc_free", (b.v, 0))
                tv = ln_norm(srcc, xqsrc, [tlx])
                if debug and ft == 0 and tc == 0:
                    b.store([tlx], dbg["x1f0"][:, :], x1f[:, 0:D])
                tq = quantize_rows(xqsrc, D, xqo, [tv, pv("xqo_free")], scrF)
                P["xqsrc_free"] = tq
                if debug and ft == 0 and tc == 0:
                    P["xqsrc_free"] = b.store([tq], dbg["xn2c0"][:, :], xqsrc)
                tt = tr_group(xqo, 8, pb2, [tq, pv("pb2F")])
                P["xqo_free"] = tt
                te = b.emit("v", [tt], lambda h, tc=tc: h.tensor_copy(
                    xqT3[:, :, tc * CH:(tc + 1) * CH],
                    pb2.rearrange("p (dc i) -> p dc i", i=128)))
                P["pb2F"] = te
                tok_xqT.append(te)
                if debug and ft == 0 and tc == 0:
                    P["xqo_free"] = b.store([tq], dbg["xq0"][:, :], xqo)
            pieces = [(i * 512, 512) for i in range(5)] + [(2560, 256)]
            for (i0, fw) in pieces:
                piece = i0 // 512
                tlg = b.load([pv("wfA_free")], wfA[:, 0:8 * fw],
                             wgq_s[piece, :, 0:8 * fw])
                tlu = b.load([pv("wfB_free")], wfB[:, 0:8 * fw],
                             wuq_s[piece, :, 0:8 * fw])
                wg3 = wfA.rearrange("p (dc i) -> p dc i", i=512)
                wu3 = wfB.rearrange("p (dc i) -> p dc i", i=512)
                for tc in range(4):
                    def f_gu(h, tc=tc, fw=fw, i0=i0):
                        last = None
                        for dc in range(8):
                            last = h.matmul(pb2[:, 0:fw],
                                            xqT3[:, dc, tc * CH:(tc + 1) * CH],
                                            wfA[:, dc * fw:(dc + 1) * fw] if False else
                                            wfA.rearrange("p (dc i) -> p dc i", i=fw)[:, dc, :],
                                            start=(dc == 0), stop=(dc == 7))
                        for dc in range(8):
                            last = h.matmul(pb2[:, 512:512 + fw],
                                            xqT3[:, dc, tc * CH:(tc + 1) * CH],
                                            wfB.rearrange("p (dc i) -> p dc i", i=fw)[:, dc, :],
                                            start=(dc == 0), stop=(dc == 7))
                        return last
                    tt = b.emit("t", tok_xqT + [tlg, tlu, pv("pb2F")], f_gu)
                    tsl = b.emit("s", [tt, pv("silb_free")], lambda h, fw=fw: h.activation(
                        out=silb[:, 0:fw], in_=pb2[:, 0:fw], func=AF.Silu))
                    tv = b.emit("v", [tsl, pv("hb_free")], lambda h, fw=fw: h.tensor_mul(
                        hb[:, 0:fw], pb2[:, 512:512 + fw], silb[:, 0:fw]))
                    P["pb2F"] = tv
                    P["silb_free"] = tv
                    if debug and ft == 0 and i0 == 0 and tc == 0:
                        P["hb_free2"] = b.store([tv], dbg["hb0"][:, :], hb[:, 0:512])
                        P["silb_free"] = b.store([tv], dbg["sil0"][:, :], silb[:, 0:512])
                    tq = quantize_rows(hb[:, 0:fw], fw, hqb[:, 0:fw],
                                       [tv, pv("hqb_free"), pv("hb_free2")], scrF)
                    P["hb_free"] = tq
                    nbl = fw // 128
                    tt = tr_group(hqb, nbl, pa, [tq, pv("paF")])
                    P["hqb_free"] = tt
                    te = b.emit("v", [tt], lambda h, i0=i0, tc=tc, nbl=nbl: h.tensor_copy(
                        hqT3[:, i0 // 128:i0 // 128 + nbl, tc * CH:(tc + 1) * CH],
                        pa.rearrange("p (j i) -> p j i", i=128)[:, 0:nbl, :]))
                    P["paF"] = te
                    if debug and ft == 0 and i0 == 0 and tc == 0:
                        P["hqb_free"] = b.store([tq], dbg["hq0"][:, :], hqb[:, 0:512])
                P["wfA_free"] = tt
                P["wfB_free"] = tt
            tok_hqT = te
            if debug and ft == 0:
                b.store([te], dbg["hqT0"][:, :], hqTf)
                b.store([tok_xqT[-1]], dbg["xqT0"][:, :], xqT)
            tdown = None
            for ic in range(N_IC):
                wdq = wdq0 if ic % 2 == 0 else wdq1
                tlw = b.load([pv(f"wdq{ic%2}_free")], wdq, wdq_s[ic * 128:(ic + 1) * 128, :])
                deps = [tlw, tok_hqT]
                if ic == 0:
                    deps += [pv("pb2F"), pv("poF"), pv("paF"), pv("poB"), pv("ps")]
                def f_dn(h, ic=ic, wdq=wdq):
                    last = None
                    for tc in range(4):
                        for fh in range(2):
                            if tc < 2:
                                dstp = po[:, tc * 1024 + fh * 512:
                                          tc * 1024 + (fh + 1) * 512]
                            elif tc == 2:
                                dstp = pb2[:, fh * 512:(fh + 1) * 512]
                            else:
                                dstp = pa[:, 0:512] if fh == 0 else ps[:, 0:512]
                            last = h.matmul(dstp,
                                            hqT3[:, ic, tc * CH:(tc + 1) * CH],
                                            wdq[:, fh * 512:(fh + 1) * 512],
                                            start=(ic == 0), stop=(ic == N_IC - 1))
                    return last
                tdown = b.emit("t", deps, f_dn)
                P[f"wdq{ic%2}_free"] = tdown
            for tc in range(4):
                if tc < 2:
                    tv = b.emit("v", [tdown, pv("outv_free")], lambda h, tc=tc: h.tensor_add(
                        outv, x1f[:, tc * D:(tc + 1) * D], po[:, tc * 1024:(tc + 1) * 1024]))
                elif tc == 2:
                    tv = b.emit("v", [tdown, pv("outv_free")], lambda h, tc=tc: h.tensor_add(
                        outv, x1f[:, tc * D:(tc + 1) * D], pb2[:, :]))
                else:
                    tv = b.emit("v", [tdown, pv("outv_free")], lambda h, tc=tc: (
                        h.tensor_add(outv[:, 0:512], x1f[:, tc * D:tc * D + 512],
                                     pa[:, 0:512]),
                        h.tensor_add(outv[:, 512:1024], x1f[:, tc * D + 512:(tc + 1) * D],
                                     ps[:, 0:512]))[-1])
                tso = b.store([tv], y_out[t0 + tc * CH:t0 + (tc + 1) * CH, :], outv)
                P["outv_free"] = tso
            P["x1f_free"] = tv
            P["poB"] = tv
            P["poF"] = tv
            P["pb2F"] = tv
            P["paF"] = tv
            P["ps"] = tv

        b.final_wait()
    return nc, dbg


def _consts():
    j = np.arange(CH)[:, None]
    t = np.arange(CH)[None, :]
    ctrl = np.where(j <= t, np.float32(-1.0 / 32.0), np.float32(0.0)).astype(np.float32)
    mask = (j <= t).astype(np.float32)
    idm = np.eye(128, dtype=np.float32)
    return ctrl, mask, idm


def swz(wT, fw):
    Din, Dout = wT.shape
    npc, ndc = Dout // fw, Din // 128
    out = np.empty((npc, 128, ndc * fw), np.float32)
    w3 = wT.reshape(ndc, 128, Dout)
    for i in range(npc):
        out[i] = w3[:, :, i * fw:(i + 1) * fw].transpose(1, 0, 2).reshape(128, ndc * fw)
    return np.ascontiguousarray(out)


def swz_rows(wT, npieces):
    Din, Dout = wT.shape
    rp = Din // npieces
    ndc = rp // 128
    out = np.empty((npieces, 128, ndc * Dout), np.float32)
    for i in range(npieces):
        out[i] = wT[i * rp:(i + 1) * rp].reshape(ndc, 128, Dout).transpose(
            1, 0, 2).reshape(128, ndc * Dout)
    return np.ascontiguousarray(out)


def make_inputs(full, core):
    (x, Wq, Wk, Wv, Wg, Wgk1, Wgk2, bgk2, Wout, rms_w, w_gate, w_up, w_down) = full
    bsel, seg = core // 4, core % 4
    ctrl, mask, idm = _consts()
    C = np.ascontiguousarray
    wgk2Ta = np.zeros((128, DK), np.float32)
    wgk2Ta[0:32] = Wgk2.T
    wgk2Ta[32] = bgk2
    mk = np.zeros((128, 4), np.float32)
    for k in range(4):
        mk[:, k] = 1.0 if k < seg else 0.0
    return {
        "x": C(x[bsel, seg * T_CORE:(seg + 1) * T_CORE]),
        "wqT": swz(Wq.T, 512), "wkT": swz(Wk.T, 512),
        "wvT": swz(Wv.T, 512), "wgT": swz(Wg.T, 512),
        "wgk1T": C(Wgk1.T), "wgk2Ta": wgk2Ta, "woutT": swz_rows(Wout.T, 2),
        "rmsw2": np.ascontiguousarray(np.tile(rms_w, NH).astype(np.float32).reshape(16, 128).T),
        "wg_nat": C(w_gate), "wu_nat": C(w_up), "wd_nat": C(w_down),
        "ctrlneg": ctrl, "maskJT": mask, "ident": idm,
        "mk": mk, "mki": (1.0 - mk).astype(np.float32),
        "cvals": np.tile(np.array([EPS, 1.0 / 512.0, -0.5, 1.0 / 1024.0, 1e-12, -1e-12, 1.0 / 6.0, 0],
                                  np.float32), (128, 1)),
    }


_prog_cache = {}


def kernel(L0_attn_x, L0_attn_Wq, L0_attn_Wk, L0_attn_Wv, L0_attn_Wg,
           L0_attn_Wgk1, L0_attn_Wgk2, L0_attn_bgk2, L0_attn_Wout,
           L0_attn_rms_w, L0_ffn_w_gate, L0_ffn_w_up, L0_ffn_w_down):
    full = tuple(np.asarray(a, dtype=np.float32) for a in (
        L0_attn_x, L0_attn_Wq, L0_attn_Wk, L0_attn_Wv, L0_attn_Wg, L0_attn_Wgk1,
        L0_attn_Wgk2, L0_attn_bgk2, L0_attn_Wout, L0_attn_rms_w,
        L0_ffn_w_gate, L0_ffn_w_up, L0_ffn_w_down))
    if "nc" not in _prog_cache:
        _prog_cache["nc"], _ = build_program(False)
    nc = _prog_cache["nc"]
    in_maps = [make_inputs(full, c) for c in range(8)]
    res = run_bass_kernel_spmd(nc, in_maps, core_ids=list(range(8)))
    out = np.empty((2, 8192, D), np.float32)
    for c in range(8):
        bsel, seg = c // 4, c % 4
        out[bsel, seg * T_CORE:(seg + 1) * T_CORE] = res.results[c]["y"]
    return out


if __name__ == "__main__":
    nc, _ = build_program(False)
    print("built OK")
